# version 06
# baseline (speedup 1.0000x reference)
import numpy as np
import ml_dtypes
import concourse.bass as bass
import concourse.mybir as mybir
from concourse.bass_utils import run_bass_kernel_spmd

F32 = mybir.dt.float32
BF16 = mybir.dt.bfloat16
AF = mybir.ActivationFunctionType
OP = mybir.AluOpType

D = 1024
NKC = D // 128
HID = 4096
NHC = HID // 128
TQ = 512
NPAIR = 4
GT = 512
NM2 = NHC // 2
EPS = 1e-5

SEMS = ["SY", "LDX0", "LDX1", "XST0", "XST1", "RTB0", "RTB1", "WQL", "XNG",
        "W1S0", "W1S1", "W2S0", "W2S1", "WB0", "WB1", "CAND", "XOW",
        "SYT0", "SYT1", "GPS0", "GPS1", "GPP0", "GPP1", "LNS0", "LNS1",
        "GAT0", "GAT1", "FHA", "HAL", "DOUT", "VE", "TE", "SC", "GPC", "CCS", "PHZ"]


class Ctx:
    def __init__(self, eng=None, builder=None, plan=None, semh=None):
        self.eng = eng
        self.b = builder
        self.plan = plan
        self.semh = semh
        self.sem = {s: 0 for s in SEMS}
        self.marks = {} if plan is None else plan.marks

    SELF_SEM = {"vector": "VE", "scalar": "SC", "gpsimd": "GPC"}

    def op(self, eng, fn, inc=None, mark=None, waits=()):
        if self.eng == eng:
            for w in waits:
                self._wait(w)
            ss = self.SELF_SEM.get(eng)
            if ss is not None and self.sem[ss] > 0 and inc is not None and inc[0] == ss:
                self.b.wait_ge(self.semh[ss], self.sem[ss])
        ins = fn(self.b) if self.eng == eng else None
        if inc is not None:
            s, k = inc
            self.sem[s] += k
            if ins is not None:
                ins.then_inc(self.semh[s], k)
            if mark is not None and self.plan is None:
                assert mark not in self.marks, f"dup mark {mark}"
                self.marks[mark] = (s, self.sem[s])
        return ins

    def wait(self, eng, *ws):
        if self.eng == eng:
            for w in ws:
                self._wait(w)

    def _wait(self, w):
        if isinstance(w, tuple) and len(w) == 3 and w[0] == "raw":
            self.b.wait_ge(self.semh[w[1]], w[2])
        else:
            s, v = self.marks[w]
            self.b.wait_ge(self.semh[s], v)


def build_program(T, dbg=False):
    TOWN = T // 2
    NT = T // TQ
    NLN_A = T // 128
    NLN_D = TOWN // 128
    NG = TOWN // GT
    NT128 = GT // 128
    NWBU = 2 * NM2 + 4

    nc = bass.Bass("TRN2", target_bir_lowering=False, detect_race_conditions=False)

    def din(name, shape, dt=F32):
        return nc.dram_tensor(name, shape, dt, kind="ExternalInput")

    WSHAPE = {"wq": (512, D), "wk": (512, D), "wv": (512, D),
              "w1": (HID, D), "w2": (HID, D), "w3": (D, HID),
              "h1": (HID, HID), "h2": (HID, HID)}
    x_nat = din("x_nat", [T, D])
    x_own = din("x_own", [TOWN, D])
    wsrc = {k: din(k, list(v)) for k, v in WSHAPE.items()}
    bqI = din("bq", [128, NPAIR]); bkI = din("bk", [128, NPAIR]); bvI = din("bv", [128, NPAIR])
    s1I = din("s1c", [128, NHC]); s2I = din("s2c", [128, NHC])
    sh1I = din("sh1c", [128, NHC]); sh2I = din("sh2c", [128, NHC])
    s3bI = din("s3b", [128, D])
    ropeI = din("ropeT", [2, 128, T])
    masksI = din("masksI", [4, 128, TQ])
    identBI = din("identB", [128, 128], BF16)
    identFI = din("identF", [128, 128], F32)
    ptmI = din("ptmI", [128, 128], BF16)
    mblI = din("mblI", [128, 2])
    out_shard = nc.dram_tensor("out_shard", [TOWN, D], F32, kind="ExternalOutput")
    if dbg:
        dbg_attn = nc.dram_tensor("dbg_attn", [512, T], BF16, kind="ExternalOutput")
        dbg_gth = nc.dram_tensor("dbg_gth", [1024, T], BF16, kind="ExternalOutput")
        dbg_xnT = nc.dram_tensor("dbg_xnT", [D, T], BF16, kind="ExternalOutput")
        dbg_xnOT = nc.dram_tensor("dbg_xnOT", [D, TOWN], BF16, kind="ExternalOutput")

    xnT_scr = nc.dram_tensor("xnT_scr", [D, T], BF16)
    xnOT_scr = nc.dram_tensor("xnOT_scr", [D, TOWN], BF16)
    attn_scr = nc.dram_tensor("attn_scr", [512, T], BF16)
    gth = nc.dram_tensor("gth", [1024, T], BF16)
    hA_scr = nc.dram_tensor("hA_scr", [TOWN // GT, 128, NHC * GT], BF16)
    bf_scr = {k: nc.dram_tensor(k + "_bf", list(v), BF16) for k, v in WSHAPE.items()}
    tr_scr = {}
    for k, v in WSHAPE.items():
        if k in ("wq", "wk", "wv"):
            tr_scr[k] = nc.dram_tensor(k + "T_s", [v[1], v[0]], BF16)
        else:
            tr_scr[k] = nc.dram_tensor(k + "T_s", [v[0] // 256, 128, v[1] // 128, 256], BF16)

    def sb(name, shape, dt, **kw):
        return nc.alloc_sbuf_tensor(name, shape, dt, **kw)

    ident = sb("identS", [128, 128], BF16)
    identF = sb("identFS", [128, 128], F32)
    ptm = sb("ptmS", [128, 128], BF16)
    bq_sb = sb("bq_sb", [128, NPAIR], F32)
    bk_sb = sb("bk_sb", [128, NPAIR], F32)
    bv_sb = sb("bv_sb", [128, NPAIR], F32)
    s1_sb = sb("s1_sb", [128, NHC], F32)
    s2_sb = sb("s2_sb", [128, NHC], F32)
    s12_sb = sb("s12_sb", [128, NHC], F32)
    sh1_sb = sb("sh1_sb", [128, NHC], F32)
    sh2_sb = sb("sh2_sb", [128, NHC], F32)
    s3b_sb = sb("s3b_sb", [128, D], F32)
    mbl_sb = sb("mbl_sb", [128, 2], F32)
    xln = [sb(f"xln{j}", [128, D], F32) for j in range(2)]
    sqd = sb("sqd", [128, D], F32)
    xnb = [sb(f"xnb{j}", [128, D], F32) for j in range(2)]
    cols = sb("cols", [128, 16], F32)
    trstage = [sb(f"trstage{j}", [128, D], BF16) for j in range(2)]

    l0_top = nc.sbuf_base

    pt_b = [sb(f"pt_b{j}", [128, HID], BF16, is_transpose_dest=True) for j in range(2)]
    masks_sb = sb("masks_sb", [128, 4 * TQ], F32)
    xst = [sb(f"xst{j}", [128, NKC * TQ], BF16) for j in range(2)]
    rtb = [sb(f"rtb{j}", [128, 2 * TQ], F32) for j in range(2)]
    wqT_sb = sb("wqT_sb", [128, NKC * 128], BF16)
    wkT_sb = sb("wkT_sb", [128, NKC * 128], BF16)
    wvT_sb = sb("wvT_sb", [128, NKC * 128], BF16)
    qT = sb("qT", [128, T], BF16)
    kT = sb("kT", [128, T], BF16)
    v_sb = sb("v_sb", [128, T], BF16)
    q_sbuf = [sb(f"q_sbuf{j}", [128, TQ], BF16) for j in range(2)]
    k_sbuf = [sb(f"k_sbuf{j}", [128, TQ], BF16) for j in range(2)]
    vT_sbuf = [sb(f"vT_sbuf{j}", [128, TQ], F32) for j in range(2)]
    t1q = [sb(f"t1q{j}", [128, TQ], F32) for j in range(2)]
    t2q = [sb(f"t2q{j}", [128, TQ], F32) for j in range(2)]
    t1k = [sb(f"t1k{j}", [128, TQ], F32) for j in range(2)]
    t2k = [sb(f"t2k{j}", [128, TQ], F32) for j in range(2)]
    sA_sb = [sb(f"sA_sb{j}", [128, TQ], BF16) for j in range(3)]
    sB_sb = [sb(f"sB_sb{j}", [128, TQ], BF16) for j in range(3)]
    avst = [sb(f"avst{j}", [128, TQ], BF16) for j in range(2)]
    fxng = sb("fxng", [128, NKC * GT], BF16)
    fw1s = [sb(f"fw1s{j}", [128, NKC * 256], BF16) for j in range(2)]
    fw2s = [sb(f"fw2s{j}", [128, NKC * 256], BF16) for j in range(2)]
    fsg = [sb(f"fsg{j}", [128, GT], F32) for j in range(2)]
    ftm1 = [sb(f"ftm1{j}", [128, GT], F32) for j in range(2)]
    ftm2 = [sb(f"ftm2{j}", [128, GT], F32) for j in range(2)]
    fst = [sb(f"fst{j}", [128, GT], BF16) for j in range(2)]

    l2ptr = [l0_top]

    def sb2(name, shape, dt):
        nbytes = int(np.prod(shape[1:])) * (2 if dt == BF16 else 4)
        off = (l2ptr[0] + 31) // 32 * 32
        t = nc.alloc_sbuf_tensor_at(name, shape, dt, offset=off)
        l2ptr[0] = off + nbytes
        return t

    xng = [sb2("xng0", [128, NKC * GT], BF16)]
    hA = sb2("hA", [128, NHC * GT], BF16)
    hB = sb2("hB", [128, NHC * GT], BF16)
    wbig = [sb2(f"wbig{j}", [128, NHC * 256], BF16) for j in range(2)]
    xown_sb = [sb2(f"xown_sb{j}", [128, D], F32) for j in range(NT128)]
    candA = [sb2("candA0", [128, D], BF16)]
    candB = [sb2("candB0", [128, D], BF16)]
    bln = [sb2("bln0", [128, D], F32)]
    atok = [sb2(f"atok{j}", [128, D], BF16) for j in range(NT128)]
    outst = [sb2(f"outst{j}", [128, D], F32) for j in range(NT128)]
    o1b = [sb2(f"o1b{j}", [128, 256], F32) for j in range(2)]
    o2b = [sb2(f"o2b{j}", [128, 256], F32) for j in range(2)]
    assert l2ptr[0] <= 229376, f"L2 overflow {l2ptr[0]}"

    pb = [nc.alloc_psum_tensor(f"pb{i}", [128, 512], F32) for i in range(6)]
    ptr_ = [nc.alloc_psum_tensor("ptA", [128, 512], F32),
            nc.alloc_psum_tensor("ptB", [128, 512], F32)]

    semh = {s: nc.alloc_semaphore(s) for s in SEMS}

    def walk(c: Ctx):
        cl = [(ident, identBI), (identF, identFI), (ptm, ptmI), (bq_sb, bqI),
              (bk_sb, bkI), (bv_sb, bvI), (s1_sb, s1I), (s2_sb, s2I),
              (sh1_sb, sh1I), (sh2_sb, sh2I), (s3b_sb, s3bI), (mbl_sb, mblI)]
        for dst_, src_ in cl:
            c.op("sync", lambda sy, dst_=dst_, src_=src_: sy.dma_start(out=dst_[:], in_=src_[:]),
                 inc=("SY", 16))
        c.op("sync", lambda sy: sy.dma_start(
                out=masks_sb.rearrange("p (j t) -> p j t", j=4),
                in_=masksI.rearrange("j p t -> p j t")),
             inc=("SY", 16), mark="consts")
        c.op("vector", lambda v: v.tensor_mul(s12_sb[:], s1_sb[:], s2_sb[:]),
             inc=("VE", 1), mark="s12", waits=["consts"])

        p1_done = {}
        for wn in ["wq", "wk", "wv"]:
            src, dst = wsrc[wn], bf_scr[wn]
            R, C = WSHAPE[wn]
            nrc = R // 128
            for rc in range(nrc):
                par = rc % 2
                sname = f"GPP{par}"
                thr = [("raw", sname, c.sem[sname])] if c.sem[sname] > 0 else []
                mk = f"p1_{wn}_{rc}" if rc >= nrc - 2 else None
                c.op("gpsimd", lambda g, src=src, dst=dst, rc=rc: g.dma_start(
                        out=dst[rc * 128:(rc + 1) * 128, :],
                        in_=src[rc * 128:(rc + 1) * 128, :]),
                     inc=(sname, 16), mark=mk, waits=thr)
            p1_done[wn] = [f"p1_{wn}_{nrc-2}", f"p1_{wn}_{nrc-1}"]
        for wn in ["w1", "w2", "w3", "h1", "h2"]:
            src, dst = wsrc[wn], bf_scr[wn]
            R, C = WSHAPE[wn]
            for rc in range(R // 128):
                c.op("gpsimd", lambda g, src=src, dst=dst, rc=rc: g.dma_start(
                        out=dst[rc * 128:(rc + 1) * 128, :],
                        in_=src[rc * 128:(rc + 1) * 128, :]),
                     inc=(f"GPP{rc % 2}", 16))
        p1_all = [("raw", "GPP0", c.sem["GPP0"]), ("raw", "GPP1", c.sem["GPP1"])]
        for wn in ["w1", "w2", "w3", "h1", "h2"]:
            p1_done[wn] = p1_all
        prep_sy = []
        p2_done = {}
        p2_end = {}
        p1_cum = 0
        p1_cum_map = {}
        for wn in ["wq", "wk", "wv", "w1", "w2", "w3", "h1", "h2"]:
            p1_cum += WSHAPE[wn][0] // 128
            p1_cum_map[wn] = p1_cum
        for wn in ["wq", "wk", "wv", "w1", "w2", "w3", "h1", "h2"]:
            dst_bf = bf_scr[wn]
            dstT = tr_scr[wn]
            R, C = WSHAPE[wn]
            for ic in range(C // 128):
                idx = len(prep_sy)
                j = idx % 2
                def fl(sy, dst_bf=dst_bf, ic=ic, j=j, R=R):
                    return sy.dma_start(out=pt_b[j][:, 0:R],
                                        in_=dst_bf[:, ic * 128:(ic + 1) * 128],
                                        transpose=True)
                if wn in ("wq", "wk", "wv"):
                    def fs(g, dstT=dstT, ic=ic, j=j, R=R):
                        return g.dma_start(out=dstT[ic * 128:(ic + 1) * 128, :],
                                           in_=pt_b[j][:, 0:R])
                else:
                    def fs(g, dstT=dstT, ic=ic, j=j, R=R):
                        return g.dma_start(
                            out=dstT[:, :, ic, :].rearrange("m p oc -> p m oc"),
                            in_=pt_b[j][:, 0:R].rearrange("p (m oc) -> p m oc", oc=256))
                lw = list(p1_done[wn])
                if idx >= 2:
                    lw.append(f"p2s_{idx-2}")
                prep_sy.append((fl, f"p2l_{idx}", lw, fs, f"p2s_{idx}", p1_cum_map[wn]))
                if wn not in p2_done:
                    p2_done[wn] = []
                p2_done[wn] = (p2_done[wn] + [f"p2s_{idx}"])[-2:]
                p2_end[wn] = idx + 1
        pst = {"gp": 0, "sy": 0}

        pst_cap = [24]

        def drain_until(wn):
            pst_cap[0] = 10**9
            while pst["sy"] < p2_end[wn]:
                before = (pst["gp"], pst["sy"])
                drain_prep(4, 4)
                assert (pst["gp"], pst["sy"]) != before, "prep drain stuck"

        def drain_prep(n_gp, n_sy):
            for _ in range(n_sy):
                i = pst["sy"]
                if i >= len(prep_sy) or i >= pst_cap[0]:
                    break
                fl, mkl, lw, fs, mks, min_gp = prep_sy[i]
                c.op("sync", fl, inc=(f"SYT{i % 2}", 16), mark=mkl, waits=lw)
                c.op("sync", fs, inc=(f"GPS{i % 2}", 16), mark=mks, waits=[mkl])
                pst["sy"] += 1

        def ln_tile(l, src_dram, row0, dst_scr, col0, sl, p2l_, p1l_):
            j = sl % 2
            cS = cols[:, j * 8 + 0:j * 8 + 1]
            cQ = cols[:, j * 8 + 1:j * 8 + 2]
            cM = cols[:, j * 8 + 2:j * 8 + 3]
            cX = cols[:, j * 8 + 3:j * 8 + 4]
            cN = cols[:, j * 8 + 4:j * 8 + 5]
            cV = cols[:, j * 8 + 5:j * 8 + 6]
            cU = cols[:, j * 8 + 6:j * 8 + 7]
            cR = cols[:, j * 8 + 7:j * 8 + 8]
            lw = [f"ln_ap_{p2l_}"] if p2l_ is not None else ["consts"]
            c.op("sync", lambda sy: sy.dma_start(out=xln[j][:], in_=src_dram[row0:row0 + 128, :]),
                 inc=(f"LDX{j}", 16), mark=f"ln_ld_{l}", waits=lw)
            c.op("vector", lambda v: v.reduce_sum(cS, xln[j][:], mybir.AxisListType.X),
                 inc=("VE", 1), mark=f"ln_sum_{l}", waits=[f"ln_ld_{l}"])
            scw = [f"ln_ld_{l}"] + ([f"ln_cm_{p2l_}"] if p2l_ is not None else [])
            c.op("scalar", lambda s: s.activation(sqd[:], xln[j][:], AF.Square, accum_out=cQ),
                 inc=("SC", 1), mark=f"ln_sq_{l}", waits=scw)
            vcw = [f"ln_sq_{l}"] + ([f"ln_rs_{p2l_}", f"ln_ap_{p2l_}"] if p2l_ is not None else [])
            c.op("vector", lambda v: v.tensor_scalar_mul(cM, cS, 1.0 / D),
                 inc=("VE", 1), waits=vcw)
            c.op("vector", lambda v: v.tensor_scalar_mul(cN, cQ, 1.0 / D), inc=("VE", 1))
            c.op("vector", lambda v: v.tensor_mul(cX, cM, cM), inc=("VE", 1))
            c.op("vector", lambda v: v.tensor_sub(cV, cN, cX),
                 inc=("VE", 1), mark=f"ln_cm_{l}")
            rsw = [f"ln_cm_{l}"] + ([f"ln_ap_{p2l_}"] if p2l_ is not None else [])
            c.op("scalar", lambda s: s.activation(cU, cV, AF.Ln, bias=np.float32(EPS)),
                 inc=("SC", 1), waits=rsw)
            c.op("scalar", lambda s: s.activation(cR, cU, AF.Exp, scale=np.float32(-0.5)),
                 inc=("SC", 1), mark=f"ln_rs_{l}")
            apw = [f"ln_rs_{l}"] + ([f"ln_tr_{p2l_}"] if p2l_ is not None else [])
            c.op("vector",
                 lambda v: v.tensor_scalar(xnb[j][:], xln[j][:], cM, cR, OP.subtract, OP.mult),
                 inc=("VE", 1), mark=f"ln_ap_{l}", waits=apw)
            trw = [f"ln_ap_{l}"] + ([f"ln_evA_{p1l_}", f"ln_evB_{p1l_}"]
                                     if p1l_ is not None else [])

            def trans(t):
                ins = None
                for kc2 in range(NKC):
                    pt = ptr_[kc2 // 4]
                    ins = t.transpose(pt[:, (kc2 % 4) * 128:(kc2 % 4) * 128 + 128],
                                      xnb[j][:, kc2 * 128:(kc2 + 1) * 128], identF[:])
                return ins
            c.op("tensor", trans, inc=("TE", 1), mark=f"ln_tr_{l}", waits=trw)
            evw = [f"ln_tr_{l}"] + ([f"ln_sc_{p2l_}"] if p2l_ is not None else [])
            c.op("vector", lambda v: v.tensor_copy(trstage[j][:, 0:512], ptr_[0][:]),
                 inc=("VE", 1), mark=f"ln_evA_{l}", waits=evw)
            c.op("scalar", lambda s_: s_.copy(trstage[j][:, 512:1024], ptr_[1][:]),
                 inc=("SC", 1), mark=f"ln_evB_{l}", waits=evw)
            c.op("gpsimd",
                 lambda g: g.dma_start(
                     out=dst_scr.rearrange("(kc p) t -> p kc t", p=128)[:, :, col0:col0 + 128],
                     in_=trstage[j].rearrange("p (kc t) -> p kc t", kc=NKC)),
                 inc=(f"LNS{j}", 16), mark=f"ln_sc_{l}",
                 waits=[f"ln_evA_{l}", f"ln_evB_{l}"])

        fee = {"i": 0, "last_ln": None}
        NGf = TOWN // GT

        def feeder_unit(u):
            g, m2 = divmod(u, NM2)
            mj = m2 % 2
            if m2 == 0:
                xw = [f"ln_sc_{NLN_A + g*4 + 3}"]
                if g > 0:
                    xw.append(f"f_w2m_{(g-1)*2*NM2 + 2*NM2 - 1}")
                c.op("sync", lambda sy, g=g: sy.dma_start(
                        out=fxng.rearrange("p (kc t) -> p kc t", kc=NKC),
                        in_=xnOT_scr.rearrange("(kc p) t -> p kc t", p=128)[:, :, g * GT:(g + 1) * GT]),
                     inc=("XNG", 16), mark=f"f_xn_{g}", waits=xw)
            wl = list(p2_done["w1"]) if u == 0 else []
            if u >= 2:
                wl.append(f"f_w1m_{(u-2)*2 + 1}")
            c.op("sync", lambda sy, mj=mj, m2=m2: sy.dma_start(
                    out=fw1s[mj].rearrange("p (kc o) -> p kc o", kc=NKC),
                    in_=tr_scr["w1"][m2]),
                 inc=(f"W1S{mj}", 16), mark=f"f_w1l_{u}", waits=wl)
            wl = list(p2_done["w2"]) if u == 0 else []
            if u >= 2:
                wl.append(f"f_w2m_{(u-2)*2 + 1}")
            c.op("sync", lambda sy, mj=mj, m2=m2: sy.dma_start(
                    out=fw2s[mj].rearrange("p (kc o) -> p kc o", kc=NKC),
                    in_=tr_scr["w2"][m2]),
                 inc=(f"W2S{mj}", 16), mark=f"f_w2l_{u}", waits=wl)
            for mm in range(2):
                m = m2 * 2 + mm
                hm = u * 2 + mm
                mw = [f"f_w1l_{u}", f"f_xn_{g}"]
                if hm == 0:
                    mw += [f"ln_evA_{fee['last_ln']}", f"ln_evB_{fee['last_ln']}"]
                else:
                    mw.append(f"f_t_{hm-1}")

                def w1m(t, mj=mj, mm=mm):
                    ins = None
                    for kc in range(NKC):
                        ins = t.matmul(ptr_[0][:],
                                       fw1s[mj][:, kc * 256 + mm * 128:kc * 256 + mm * 128 + 128],
                                       fxng[:, kc * GT:(kc + 1) * GT],
                                       start=(kc == 0), stop=(kc == NKC - 1))
                    return ins
                c.op("tensor", w1m, inc=("TE", 1), mark=f"f_w1m_{hm}", waits=mw)
                mw = [f"f_w2l_{u}"]
                if hm > 0:
                    mw.append(f"f_t2_{hm-1}")

                def w2m(t, mj=mj, mm=mm):
                    ins = None
                    for kc in range(NKC):
                        ins = t.matmul(ptr_[1][:],
                                       fw2s[mj][:, kc * 256 + mm * 128:kc * 256 + mm * 128 + 128],
                                       fxng[:, kc * GT:(kc + 1) * GT],
                                       start=(kc == 0), stop=(kc == NKC - 1))
                    return ins
                c.op("tensor", w2m, inc=("TE", 1), mark=f"f_w2m_{hm}", waits=mw)
                sgw = [f"f_w1m_{hm}"]
                if hm >= 2:
                    sgw.append(f"f_t_{hm-2}")
                c.op("scalar", lambda s_, m=m, mm=mm: s_.activation(
                        fsg[mm][:], ptr_[0][:], AF.Sigmoid, scale=s1_sb[:, m:m + 1]),
                     inc=("SC", 1), mark=f"f_sg_{hm}", waits=sgw)
                tw_ = [f"f_sg_{hm}"]
                if hm >= 2:
                    tw_.append(f"f_t2_{hm-2}")
                c.op("vector", lambda v, mm=mm: v.tensor_mul(
                        ftm1[mm][:], ptr_[0][:], fsg[mm][:]),
                     inc=("VE", 1), mark=f"f_t_{hm}", waits=tw_)
                t2w = [f"f_w2m_{hm}"]
                if hm >= 2:
                    t2w.append(f"f_hst_{hm-2}")
                c.op("vector", lambda v, mm=mm: v.tensor_mul(
                        ftm2[mm][:], ftm1[mm][:], ptr_[1][:]),
                     inc=("VE", 1), mark=f"f_t2_{hm}", waits=t2w)
                hw_ = [f"f_t2_{hm}"]
                if hm >= 2:
                    hw_.append(f"f_dma_{hm-2}")
                c.op("scalar", lambda s_, m=m, mm=mm: s_.activation(
                        fst[mm][:], ftm2[mm][:], AF.Identity, scale=s12_sb[:, m:m + 1]),
                     inc=("SC", 1), mark=f"f_hst_{hm}", waits=hw_)
                c.op("gpsimd", lambda g_, g=g, m=m, mm=mm: g_.dma_start(
                        out=hA_scr[g][:, m * GT:(m + 1) * GT], in_=fst[mm][:]),
                     inc=("FHA", 16), mark=f"f_dma_{hm}", waits=[f"f_hst_{hm}"])

        def feeder_drain(k):
            if fee["i"] == 0 and k > 0:
                drain_until("w2")
            for _ in range(k):
                if fee["i"] >= NGf * NM2:
                    return
                feeder_unit(fee["i"])
                fee["i"] += 1

        ln_order = []
        di = 0
        for a in range(NLN_A):
            ln_order.append(a)
            if a % 2 == 1 and di < NLN_D:
                ln_order.append(NLN_A + di)
                di += 1
        while di < NLN_D:
            ln_order.append(NLN_A + di)
            di += 1
        for sl, l in enumerate(ln_order):
            p2l_ = ln_order[sl - 2] if sl >= 2 else None
            p1l_ = ln_order[sl - 1] if sl >= 1 else None
            if l < NLN_A:
                ln_tile(l, x_nat, l * 128, xnT_scr, l * 128, sl, p2l_, p1l_)
            else:
                dl = l - NLN_A
                ln_tile(l, x_own, dl * 128, xnOT_scr, dl * 128, sl, p2l_, p1l_)
            drain_prep(1, 1)
        fee["last_ln"] = ln_order[-1]

        NSIG = [4 * tau + 4 for tau in range(NT)]

        def stage1_main(p, i):
            lt = p * NT + i
            jj = i % 2
            t0 = i * TQ
            xw = []
            if p == 0:
                xw.append(f"ln_sc_{4*i+3}")
            if lt >= 2:
                xw.append(f"b_vmm_{lt-2}")
            c.op("sync", lambda sy, jj=jj, t0=t0: sy.dma_start(
                    out=xst[jj].rearrange("p (kc t) -> p kc t", kc=NKC),
                    in_=xnT_scr.rearrange("(kc p) t -> p kc t", p=128)[:, :, t0:t0 + TQ]),
                 inc=(f"XST{jj}", 16), mark=f"b_xl_{lt}", waits=xw)
            rw = []
            if lt >= 2:
                rw += [f"b_t1k_{lt-2}", f"b_t2k_{lt-2}"]
            c.op("sync", lambda sy, jj=jj, t0=t0: sy.dma_start(
                    out=rtb[jj].rearrange("p (c t) -> p c t", c=2),
                    in_=ropeI[:, :, t0:t0 + TQ].rearrange("c p t -> p c t")),
                 inc=(f"RTB{jj}", 16), mark=f"b_rl_{lt}", waits=rw)
            qw = [f"b_xl_{lt}"]
            if i == 0:
                qw.append(f"b_wl_{p}")
            if lt >= 2:
                qw.append(f"b_t2q_{lt-2}")
            if i == 0 and p > 0:
                qw += [f"b_avevA_{p-1}_{NT-1}", f"b_avevB_{p-1}_{NT-1}",
                       f"b_seA_{p-1}_{NT-1}_{NSIG[NT-1]-5}"]

            def qmm(t, jj=jj):
                ins = None
                for kc in range(NKC):
                    ins = t.matmul(pb[jj][:], wqT_sb[:, kc * 128:(kc + 1) * 128],
                                   xst[jj][:, kc * TQ:(kc + 1) * TQ],
                                   start=(kc == 0), stop=(kc == NKC - 1))
                return ins
            c.op("tensor", qmm, inc=("TE", 1), mark=f"b_qmm_{lt}", waits=qw)
            kw_ = [f"b_t2k_{lt-2}"] if lt >= 2 else []

            def kmm(t, jj=jj):
                ins = None
                for kc in range(NKC):
                    ins = t.matmul(pb[2 + jj][:], wkT_sb[:, kc * 128:(kc + 1) * 128],
                                   xst[jj][:, kc * TQ:(kc + 1) * TQ],
                                   start=(kc == 0), stop=(kc == NKC - 1))
                return ins
            c.op("tensor", kmm, inc=("TE", 1), mark=f"b_kmm_{lt}", waits=kw_)
            vw = [f"b_vtrev_{lt-2}"] if lt >= 2 else []

            def vmm(t, jj=jj):
                ins = None
                for kc in range(NKC):
                    ins = t.matmul(pb[4 + jj][:], wvT_sb[:, kc * 128:(kc + 1) * 128],
                                   xst[jj][:, kc * TQ:(kc + 1) * TQ],
                                   start=(kc == 0), stop=(kc == NKC - 1))
                return ins
            c.op("tensor", vmm, inc=("TE", 1), mark=f"b_vmm_{lt}", waits=vw)
            ew1 = [f"b_qmm_{lt}"] + ([f"b_t1q_{lt-2}"] if lt >= 2 else [])
            c.op("scalar", lambda s, jj=jj, p=p: s.activation(
                    q_sbuf[jj][:], pb[jj][:], AF.Identity, bias=bq_sb[:, p:p + 1]),
                 inc=("SC", 1), mark=f"b_qe_{lt}", waits=ew1)
            ew2 = [f"b_kmm_{lt}"] + ([f"b_t1k_{lt-2}"] if lt >= 2 else [])
            c.op("scalar", lambda s, jj=jj, p=p: s.activation(
                    k_sbuf[jj][:], pb[2 + jj][:], AF.Identity, bias=bk_sb[:, p:p + 1]),
                 inc=("SC", 1), mark=f"b_ke_{lt}", waits=ew2)
            ew3 = [f"b_vmm_{lt}"] + ([f"b_vtr_{lt-2}"] if lt >= 2 else [])
            c.op("scalar", lambda s, jj=jj, p=p: s.activation(
                    vT_sbuf[jj][:], pb[4 + jj][:], AF.Identity, bias=bv_sb[:, p:p + 1]),
                 inc=("SC", 1), mark=f"b_ve_{lt}", waits=ew3)

        def stage1_lag(p, i):
            lt = p * NT + i
            jj = i % 2
            t0 = i * TQ
            c.op("tensor", lambda t, jj=jj: t.matmul(
                    pb[jj][:], ptm[:], q_sbuf[jj][:], start=True, stop=True),
                 inc=("TE", 1), mark=f"b_pq_{lt}", waits=[f"b_qe_{lt}"])
            c.op("tensor", lambda t, jj=jj: t.matmul(
                    pb[2 + jj][:], ptm[:], k_sbuf[jj][:], start=True, stop=True),
                 inc=("TE", 1), mark=f"b_pk_{lt}", waits=[f"b_ke_{lt}"])

            def vtr(t, jj=jj):
                ins = None
                for cb in range(TQ // 128):
                    ins = t.transpose(pb[4 + jj][:, cb * 128:(cb + 1) * 128],
                                      vT_sbuf[jj][:, cb * 128:(cb + 1) * 128], identF[:])
                return ins
            c.op("tensor", vtr, inc=("TE", 1), mark=f"b_vtr_{lt}", waits=[f"b_ve_{lt}"])
            gw = [f"b_qe_{lt}", f"b_rl_{lt}"] + ([f"b_qT_{lt-2}"] if lt >= 2 else [])
            c.op("gpsimd", lambda g, jj=jj: g.tensor_mul(
                    t1q[jj][:], q_sbuf[jj][:], rtb[jj][:, 0:TQ]),
                 inc=("GPC", 1), mark=f"b_t1q_{lt}", waits=gw)
            vw = [f"b_pq_{lt}", f"b_rl_{lt}"] + ([f"b_qT_{lt-2}"] if lt >= 2 else [])
            c.op("vector", lambda v, jj=jj: v.tensor_mul(
                    t2q[jj][:], pb[jj][:], rtb[jj][:, TQ:2 * TQ]),
                 inc=("VE", 1), mark=f"b_t2q_{lt}", waits=vw)
            c.op("gpsimd", lambda g, jj=jj, t0=t0: g.tensor_add(
                    qT[:, t0:t0 + TQ], t1q[jj][:], t2q[jj][:]),
                 inc=("GPC", 1), mark=f"b_qT_{lt}", waits=[f"b_t2q_{lt}"])
            gw = [f"b_ke_{lt}", f"b_rl_{lt}"] + ([f"b_kT_{lt-2}"] if lt >= 2 else [])
            c.op("gpsimd", lambda g, jj=jj: g.tensor_mul(
                    t1k[jj][:], k_sbuf[jj][:], rtb[jj][:, 0:TQ]),
                 inc=("GPC", 1), mark=f"b_t1k_{lt}", waits=gw)
            vw = [f"b_pk_{lt}"] + ([f"b_kT_{lt-2}"] if lt >= 2 else [])
            c.op("vector", lambda v, jj=jj: v.tensor_mul(
                    t2k[jj][:], pb[2 + jj][:], rtb[jj][:, TQ:2 * TQ]),
                 inc=("VE", 1), mark=f"b_t2k_{lt}", waits=vw)
            c.op("gpsimd", lambda g, jj=jj, t0=t0: g.tensor_add(
                    kT[:, t0:t0 + TQ], t1k[jj][:], t2k[jj][:]),
                 inc=("GPC", 1), mark=f"b_kT_{lt}", waits=[f"b_t2k_{lt}"])
            c.op("vector", lambda v, jj=jj, t0=t0: v.tensor_copy(
                    v_sb[:, t0:t0 + TQ], pb[4 + jj][:]),
                 inc=("VE", 1), mark=f"b_vtrev_{lt}", waits=[f"b_vtr_{lt}"])

        drain_until("wv")
        for p in range(NPAIR):
            wlw = p2_done["wq"] + p2_done["wk"] + p2_done["wv"]
            if p > 0:
                wlw.append(f"b_vmm_{p*NT-1}")
            c.op("sync", lambda sy, p=p: sy.dma_start(
                    out=wqT_sb.rearrange("p (kc o) -> p kc o", kc=NKC),
                    in_=tr_scr["wq"].rearrange("(kc p) o -> p kc o", p=128)[:, :, p * 128:(p + 1) * 128]),
                 inc=("WQL", 16), waits=wlw)
            c.op("sync", lambda sy, p=p: sy.dma_start(
                    out=wkT_sb.rearrange("p (kc o) -> p kc o", kc=NKC),
                    in_=tr_scr["wk"].rearrange("(kc p) o -> p kc o", p=128)[:, :, p * 128:(p + 1) * 128]),
                 inc=("WQL", 16))
            c.op("sync", lambda sy, p=p: sy.dma_start(
                    out=wvT_sb.rearrange("p (kc o) -> p kc o", kc=NKC),
                    in_=tr_scr["wv"].rearrange("(kc p) o -> p kc o", p=128)[:, :, p * 128:(p + 1) * 128]),
                 inc=("WQL", 16), mark=f"b_wl_{p}")

            for i in range(NT + 1):
                if i < NT:
                    stage1_main(p, i)
                if i > 0:
                    stage1_lag(p, i - 1)
                drain_prep(1, 1)

            lt_last = p * NT + NT - 1
            st_entry = [f"b_vtrev_{lt_last}", f"b_t2q_{lt_last}", f"b_t2k_{lt_last}"]

            for tau in range(NT):
                nsig = NSIG[tau]
                tq0 = tau * TQ
                tw = [f"b_qT_{p*NT+tau}", f"b_kT_{p*NT+tau}", f"b_vtrev_{p*NT+tau}"]
                if tau == 0:
                    tw += st_entry
                if tau >= 1:
                    tw += [f"b_avevA_{p}_{tau-1}", f"b_avevB_{p}_{tau-1}"]
                elif p > 0:
                    tw += [f"b_avevA_{p-1}_{NT-1}", f"b_avevB_{p-1}_{NT-1}"]

                def av_pair(t, sig, tau=tau):
                    sj = sig % 3
                    t.matmul(pb[4][0:64, :], v_sb[:, sig * 128:sig * 128 + 64],
                             sA_sb[sj][:], start=(sig == 0), stop=(sig == NSIG[tau] - 1),
                             tile_position=(0, 0))
                    return t.matmul(pb[5][64:128, :],
                                    v_sb[:, sig * 128 + 64:sig * 128 + 128],
                                    sB_sb[sj][:], start=(sig == 0), stop=(sig == NSIG[tau] - 1),
                                    tile_position=(0, 64))

                for sig in range(nsig):
                    sj = sig % 3
                    stw = list(tw) if sig == 0 else []
                    if sig == 1 and tau == 0:
                        stw = list(st_entry)
                    if sig >= 2:
                        stw += [f"b_seA_{p}_{tau}_{sig-2}", f"b_seB_{p}_{tau}_{sig-2}"]
                    elif tau > 0:
                        ps_ = NSIG[tau - 1]
                        stw += [f"b_seA_{p}_{tau-1}_{ps_-2+sig}", f"b_seB_{p}_{tau-1}_{ps_-2+sig}"]

                    def st_pair(t, sig=sig, tq0=tq0):
                        sj2 = sig % 2
                        t.matmul(pb[sj2][:], kT[0:64, sig * 128:(sig + 1) * 128],
                                 qT[0:64, tq0:tq0 + TQ], start=True, stop=True,
                                 tile_position=(0, 0))
                        return t.matmul(pb[2 + sj2][:], kT[64:128, sig * 128:(sig + 1) * 128],
                                        qT[64:128, tq0:tq0 + TQ], start=True, stop=True,
                                        tile_position=(64, 0))
                    c.op("tensor", st_pair, inc=("TE", 1), mark=f"b_st_{p}_{tau}_{sig}",
                         waits=stw)
                    is_diag = sig >= nsig - 4
                    jm = sig - (nsig - 4)
                    ew = [f"b_st_{p}_{tau}_{sig}"]
                    if sig >= 3:
                        ew.append(f"b_av_{p}_{tau}_{sig-3}")
                    elif tau > 0:
                        ew.append(f"b_av_{p}_{tau-1}_{NSIG[tau-1]-3+sig}")
                    elif p > 0:
                        ew.append(f"b_av_{p-1}_{NT-1}_{NSIG[NT-1]-3+sig}")
                    if is_diag:
                        c.op("vector", lambda v, sj=sj, jm=jm, s2=sig % 2: v.tensor_mul(
                                sA_sb[sj][:], pb[s2][:],
                                masks_sb[:, jm * TQ:(jm + 1) * TQ]),
                             inc=("VE", 1), mark=f"b_seA_{p}_{tau}_{sig}", waits=ew)
                        c.op("vector", lambda v, sj=sj, jm=jm, s2=sig % 2: v.tensor_mul(
                                sB_sb[sj][:], pb[2 + s2][:],
                                masks_sb[:, jm * TQ:(jm + 1) * TQ]),
                             inc=("VE", 1), mark=f"b_seB_{p}_{tau}_{sig}", waits=[])
                    else:
                        c.op("scalar", lambda s_, sj=sj, s2=sig % 2: s_.copy(sA_sb[sj][:], pb[s2][:]),
                             inc=("SC", 1), mark=f"b_seA_{p}_{tau}_{sig}", waits=ew)
                        c.op("vector", lambda v, sj=sj, s2=sig % 2: v.tensor_copy(sB_sb[sj][:], pb[2 + s2][:]),
                             inc=("VE", 1), mark=f"b_seB_{p}_{tau}_{sig}", waits=ew)
                    if sig >= 2:
                        c.op("tensor", lambda t, sig=sig: av_pair(t, sig - 2),
                             inc=("TE", 1), mark=f"b_av_{p}_{tau}_{sig-2}",
                             waits=[f"b_seA_{p}_{tau}_{sig-2}", f"b_seB_{p}_{tau}_{sig-2}"])
                c.op("tensor", lambda t, nsig=nsig: av_pair(t, nsig - 2),
                     inc=("TE", 1), mark=f"b_av_{p}_{tau}_{nsig-2}",
                     waits=[f"b_seA_{p}_{tau}_{nsig-2}", f"b_seB_{p}_{tau}_{nsig-2}"])
                c.op("tensor", lambda t, nsig=nsig: av_pair(t, nsig - 1),
                     inc=("TE", 1), mark=f"b_av_{p}_{tau}_{nsig-1}",
                     waits=[f"b_seA_{p}_{tau}_{nsig-1}", f"b_seB_{p}_{tau}_{nsig-1}"])
                evw = [f"b_av_{p}_{tau}_{nsig-1}"]
                if tau >= 2:
                    evw.append(f"b_gat_{p}_{tau-2}")
                elif p > 0:
                    evw.append(f"b_gat_{p-1}_{NT-2+tau}")
                c.op("vector", lambda v, tau=tau: v.tensor_copy(
                        avst[tau % 2][0:64, :], pb[4][0:64, :]),
                     inc=("VE", 1), mark=f"b_avevA_{p}_{tau}", waits=evw)
                c.op("scalar", lambda sc_, tau=tau: sc_.copy(
                        avst[tau % 2][64:128, :], pb[5][64:128, :]),
                     inc=("SC", 1), mark=f"b_avevB_{p}_{tau}", waits=evw)
                c.op("gpsimd", lambda g, tau=tau, p=p, tq0=tq0: g.dma_start(
                        out=attn_scr[p * 128:(p + 1) * 128, tq0:tq0 + TQ],
                        in_=avst[tau % 2][:]),
                     inc=(f"GAT{tau % 2}", 16), mark=f"b_gat_{p}_{tau}",
                     waits=[f"b_avevA_{p}_{tau}", f"b_avevB_{p}_{tau}"])
                drain_prep(2, 2)
                if p >= 1:
                    feeder_drain(3)

        feeder_drain(10**9)

        c.op("gpsimd", lambda g: g.collective_compute(
                "AllGather", OP.bypass,
                replica_groups=[[0, 1], [2, 3], [4, 5], [6, 7]],
                ins=[attn_scr[:]], outs=[gth[:]]),
             inc=("CCS", 1), mark="cc_done",
             waits=[("raw", "GAT0", c.sem["GAT0"]), ("raw", "GAT1", c.sem["GAT1"])])

        drain_prep(10**9, 10**9)

        c.wait("gpsimd", ("raw", "FHA", 16 * 2 * NM2 * NGf))
        for eng in ["sync", "vector", "tensor", "scalar", "gpsimd"]:
            c.op(eng, lambda e: e.nop(), inc=("PHZ", 1))
        bar = ("raw", "PHZ", 5)
        first_e = {e: True for e in ["sync", "vector", "tensor", "scalar", "gpsimd"]}

        def ew_(eng, waits):
            w = list(waits)
            if first_e[eng]:
                w.insert(0, bar)
                first_e[eng] = False
            return w

        def wbu_idx(g, kind, i):
            return g * NWBU + {"h1": 0, "h2": NM2, "w3": 2 * NM2}[kind] + i

        for g in range(NG):
            hw = []
            if g > 0:
                hw.append(f"e_w3m_{g*16 - 1}")
            c.op("sync", lambda sy, g=g: sy.dma_start(out=hA[:], in_=hA_scr[g]),
                 inc=("HAL", 16), mark=f"e_hal_{g}", waits=ew_("sync", hw))

            for m2 in range(NM2):
                u = wbu_idx(g, "h1", m2)
                mj = u % 2
                wl = p1_done["h1"] + p2_done["h1"] if u == 0 else []
                if u >= 2:
                    wl.append(f"e_wbu_mm_{u-2}")
                c.op("sync", lambda sy, mj=mj, m2=m2: sy.dma_start(
                        out=wbig[mj].rearrange("p (kc o) -> p kc o", kc=NHC),
                        in_=tr_scr["h1"][m2]),
                     inc=(f"WB{u % 2}", 16), mark=f"e_wbl_{u}", waits=wl)
                mw = [f"e_wbl_{u}", f"e_hal_{g}"]
                if m2 >= 2:
                    mw += [f"e_hB_{g}_{(m2-2)*2}", f"e_hB_{g}_{(m2-2)*2+1}"]
                elif g > 0:
                    mw += [f"e_hC_{g-1}_{NHC-2}", f"e_hC_{g-1}_{NHC-1}"]

                def h1mm(t, mj=mj, m2=m2):
                    ins = None
                    for mm in range(2):
                        bk_ = (m2 % 2) * 2 + mm
                        for kc in range(NHC):
                            ins = t.matmul(pb[bk_][:],
                                           wbig[mj][:, kc * 256 + mm * 128:kc * 256 + mm * 128 + 128],
                                           hA[:, kc * GT:(kc + 1) * GT],
                                           start=(kc == 0), stop=(kc == NHC - 1))
                    return ins
                c.op("tensor", h1mm, inc=("TE", 1), mark=f"e_wbu_mm_{u}", waits=mw)
                for mm in range(2):
                    m = m2 * 2 + mm
                    bk_ = (m2 % 2) * 2 + mm
                    sw = [f"e_wbu_mm_{u}"]
                    if g > 0 and m == 0:
                        sw.append(f"e_wbu_mm_{wbu_idx(g-1, 'h2', NM2-1)}")
                    c.op("scalar", lambda s, bk_=bk_, m=m: s.activation(
                            hB[:, m * GT:(m + 1) * GT], pb[bk_][:], AF.Gelu,
                            scale=sh1_sb[:, m:m + 1]),
                         inc=("SC", 1), mark=f"e_hB_{g}_{m}", waits=sw)

            for m2 in range(NM2):
                u = wbu_idx(g, "h2", m2)
                mj = u % 2
                wl = p1_done["h2"] + p2_done["h2"] if g == 0 and m2 == 0 else []
                if u >= 2:
                    wl.append(f"e_wbu_mm_{u-2}")
                c.op("sync", lambda sy, mj=mj, m2=m2: sy.dma_start(
                        out=wbig[mj].rearrange("p (kc o) -> p kc o", kc=NHC),
                        in_=tr_scr["h2"][m2]),
                     inc=(f"WB{u % 2}", 16), mark=f"e_wbl_{u}", waits=wl)
                mw = [f"e_wbl_{u}", f"e_hB_{g}_{NHC-1}"]
                if m2 >= 2:
                    mw += [f"e_hC_{g}_{(m2-2)*2}", f"e_hC_{g}_{(m2-2)*2+1}"]

                def h2mm(t, mj=mj, m2=m2):
                    ins = None
                    for mm in range(2):
                        bk_ = (m2 % 2) * 2 + mm
                        for kc in range(NHC):
                            ins = t.matmul(pb[bk_][:],
                                           wbig[mj][:, kc * 256 + mm * 128:kc * 256 + mm * 128 + 128],
                                           hB[:, kc * GT:(kc + 1) * GT],
                                           start=(kc == 0), stop=(kc == NHC - 1))
                    return ins
                c.op("tensor", h2mm, inc=("TE", 1), mark=f"e_wbu_mm_{u}", waits=mw)
                for mm in range(2):
                    m = m2 * 2 + mm
                    bk_ = (m2 % 2) * 2 + mm
                    sw = [f"e_wbu_mm_{u}"]
                    if m == 0:
                        sw.append(f"e_wbu_mm_{wbu_idx(g, 'h1', NM2-1)}")
                    c.op("scalar", lambda s, bk_=bk_, m=m: s.activation(
                            hA[:, m * GT:(m + 1) * GT], pb[bk_][:], AF.Gelu,
                            scale=sh2_sb[:, m:m + 1]),
                         inc=("SC", 1), mark=f"e_hC_{g}_{m}", waits=sw)

            for t_ in range(NT128):
                lidx = g * NT128 + t_
                tj = 0
                tcol = g * GT + t_ * 128
                lw = []
                if lidx == 0:
                    lw.append("cc_done")
                if lidx >= 1:
                    lw.append(f"e_bln_{lidx-1}")
                c.op("sync", lambda sy, tj=tj, tcol=tcol: sy.dma_start(
                        out=candA[tj].rearrange("p (kc t) -> p kc t", kc=NKC),
                        in_=gth.rearrange("(kc p) t -> p kc t", p=128)[:, :, tcol:tcol + 128]),
                     inc=("CAND", 16), mark=f"e_cA_{lidx}", waits=lw)
                c.op("sync", lambda sy, tj=tj, tcol=tcol: sy.dma_start(
                        out=candB[tj].rearrange("p (kc t) -> p kc t", kc=NKC),
                        in_=gth.rearrange("(kc p) t -> p kc t", p=128)[:, :, TOWN + tcol:TOWN + tcol + 128]),
                     inc=("CAND", 16), mark=f"e_cB_{lidx}", waits=[])
                gw = [f"e_cA_{lidx}", f"e_cB_{lidx}"]
                if lidx >= 1:
                    gw.append(f"e_ptr_{lidx-1}")

                c.op("vector", lambda v, tj=tj: v.tensor_scalar_mul(
                        bln[tj][:], candA[tj][:], mbl_sb[:, 0:1]),
                     inc=("VE", 1), waits=ew_("vector", gw))
                c.op("vector", lambda v, tj=tj: v.scalar_tensor_tensor(
                        bln[tj][:], candB[tj][:], mbl_sb[:, 1:2],
                        bln[tj][:], OP.mult, OP.add),
                     inc=("VE", 1), mark=f"e_bln_{lidx}")
                tw2 = [f"e_bln_{lidx}"]
                if lidx >= 1:
                    tw2 += [f"e_atokA_{lidx-1}", f"e_atokB_{lidx-1}"]
                else:
                    tw2 += [f"f_t_{2*NM2*NGf*2 - 1}" if False else
                            f"f_t2_{2*NM2*NGf - 1}", f"f_hst_{2*NM2*NGf - 1}"]

                def atr(t, tj=tj):
                    ins = None
                    for kc in range(NKC):
                        pt = ptr_[kc // 4]
                        ins = t.transpose(pt[:, (kc % 4) * 128:(kc % 4) * 128 + 128],
                                          bln[tj][:, kc * 128:(kc + 1) * 128], identF[:])
                    return ins
                c.op("tensor", atr, inc=("TE", 1), mark=f"e_ptr_{lidx}", waits=tw2)
                aw = [f"e_ptr_{lidx}"]
                if g > 0:
                    aw.append(f"e_o3_{g-1}_{t_}_3")
                c.op("vector", lambda v, t_=t_: v.tensor_copy(atok[t_][:, 0:512], ptr_[0][:]),
                     inc=("VE", 1), mark=f"e_atokA_{lidx}", waits=aw)
                c.op("scalar", lambda s_, t_=t_: s_.copy(atok[t_][:, 512:1024], ptr_[1][:]),
                     inc=("SC", 1), mark=f"e_atokB_{lidx}", waits=aw)
                xw2 = []
                if g > 0:
                    xw2.append(f"e_o2_{g-1}_{t_}_3")
                c.op("sync", lambda sy, t_=t_, tcol=tcol: sy.dma_start(
                        out=xown_sb[t_][:], in_=x_own[tcol:tcol + 128, :]),
                     inc=("XOW", 16), mark=f"e_xo_{g}_{t_}", waits=xw2)

            for qtr in range(4):
                u = wbu_idx(g, "w3", qtr)
                qj = u % 2
                wl = p1_done["w3"] + p2_done["w3"] if g == 0 and qtr == 0 else []
                if u >= 2:
                    wl.append(f"e_wbu_mm_{u-2}")
                c.op("sync", lambda sy, qj=qj, qtr=qtr: sy.dma_start(
                        out=wbig[qj].rearrange("p (kc o) -> p kc o", kc=NHC),
                        in_=tr_scr["w3"][qtr]),
                     inc=(f"WB{u % 2}", 16), mark=f"e_wbl_{u}", waits=wl)
                for t_ in range(NT128):
                    li2 = g * 16 + qtr * 4 + t_
                    bk_ = 4 + li2 % 2

                    def w3mm(t, qj=qj, t_=t_, bk_=bk_):
                        ins = None
                        for kc in range(NHC):
                            ins = t.matmul(pb[bk_][:, 0:256],
                                           hA[:, kc * GT + t_ * 128:kc * GT + t_ * 128 + 128],
                                           wbig[qj][:, kc * 256:(kc + 1) * 256],
                                           start=(kc == 0), stop=(kc == NHC - 1))
                        return ins
                    mw = [f"e_wbl_{u}", f"e_hC_{g}_{NHC-1}"]
                    if li2 >= 2:
                        mw.append(f"e_o1i_{li2-2}")
                    c.op("tensor", w3mm, inc=("TE", 1), mark=f"e_w3m_{li2}", waits=mw)
                    if t_ == NT128 - 1 and c.plan is None:
                        c.marks[f"e_wbu_mm_{u}"] = c.marks[f"e_w3m_{li2}"]
                    oj = li2 % 2
                    dcol = qtr * 256
                    c.op("vector", lambda v, oj=oj, bk_=bk_, dcol=dcol: v.tensor_mul(
                            o1b[oj][:], pb[bk_][:, 0:256], s3b_sb[:, dcol:dcol + 256]),
                         inc=("VE", 1), mark=f"e_o1i_{li2}", waits=[f"e_w3m_{li2}"])
                    o2w = [("raw", "XOW", 16 * 4 * (g + 1))]
                    if li2 >= 2:
                        pg2, r2 = divmod(li2 - 2, 16)
                        o2w.append(f"e_o3_{pg2}_{r2%4}_{r2//4}")
                    c.op("vector", lambda v, oj=oj, t_=t_, dcol=dcol: v.tensor_add(
                            o2b[oj][:], o1b[oj][:], xown_sb[t_][:, dcol:dcol + 256]),
                         inc=("VE", 1), mark=f"e_o2_{g}_{t_}_{qtr}", waits=o2w)
                    gw2 = [f"e_o2_{g}_{t_}_{qtr}",
                           f"e_atokA_{g*NT128+t_}", f"e_atokB_{g*NT128+t_}"]
                    if g > 0 and qtr == 0:
                        gw2.append(("raw", "DOUT", 16 * 4 * g))
                    c.op("gpsimd", lambda gp, oj=oj, t_=t_, dcol=dcol: gp.tensor_add(
                            outst[t_][:, dcol:dcol + 256], o2b[oj][:],
                            atok[t_][:, dcol:dcol + 256]),
                         inc=("GPC", 1), mark=f"e_o3_{g}_{t_}_{qtr}", waits=gw2)
            for t_ in range(NT128):
                trow = g * GT + t_ * 128
                c.op("sync", lambda sy, t_=t_, trow=trow: sy.dma_start(
                        out=out_shard[trow:trow + 128, :], in_=outst[t_][:]),
                     inc=("DOUT", 16), mark=f"e_out_{g}_{t_}",
                     waits=[f"e_o3_{g}_{t_}_{q}" for q in range(4)])

        if dbg:
            c.wait("sync", ("raw", "DOUT", 16 * NG * NT128))
            c.op("sync", lambda sy: sy.dma_start(out=dbg_attn[:], in_=attn_scr[:]),
                 inc=("DOUT", 16))
            c.op("sync", lambda sy: sy.dma_start(out=dbg_gth[:], in_=gth[:]),
                 inc=("DOUT", 16))
            c.op("sync", lambda sy: sy.dma_start(out=dbg_xnT[:], in_=xnT_scr[:]),
                 inc=("DOUT", 16))
            c.op("sync", lambda sy: sy.dma_start(out=dbg_xnOT[:], in_=xnOT_scr[:]),
                 inc=("DOUT", 16))
            c.wait("sync", ("raw", "DOUT", 16 * (NG * NT128 + 4)))
        else:
            c.wait("sync", ("raw", "DOUT", 16 * NG * NT128))

    plan = Ctx(semh=semh)
    walk(plan)

    with nc.Block() as block:
        block.sync(lambda e: walk(Ctx("sync", e, plan, semh)))
        block.vector(lambda e: walk(Ctx("vector", e, plan, semh)))
        block.tensor(lambda e: walk(Ctx("tensor", e, plan, semh)))
        block.scalar(lambda e: walk(Ctx("scalar", e, plan, semh)))
        block.gpsimd(lambda e: walk(Ctx("gpsimd", e, plan, semh)))

    return nc



def _host_consts(T):
    DH = 64
    inv = 1.0 / (10000.0 ** (np.arange(0, DH, 2, dtype=np.float64) / DH))
    ang = np.arange(T, dtype=np.float64)[:, None] * inv[None, :]
    cos = np.cos(ang); sin = np.sin(ang)
    cos64 = np.concatenate([cos, cos], axis=1).T
    sin64 = np.concatenate([sin, sin], axis=1).T
    ropeT = np.stack([np.tile(cos64, (2, 1)), np.tile(sin64, (2, 1))], axis=0).astype(np.float32)
    masks = np.zeros((4, 128, TQ), dtype=np.float32)
    for j in range(4):
        s = np.arange(128)[:, None] + 128 * j
        t = np.arange(TQ)[None, :]
        masks[j] = (s <= t).astype(np.float32)
    identB = np.eye(128, dtype=np.float32).astype(ml_dtypes.bfloat16)
    identF = np.eye(128, dtype=np.float32)
    P64 = np.zeros((64, 64), dtype=np.float32)
    for d in range(32):
        P64[d, d + 32] = -1.0
        P64[d + 32, d] = 1.0
    P = np.zeros((128, 128), dtype=np.float32)
    P[0:64, 0:64] = P64
    P[64:128, 64:128] = P64
    ptm = P.T.astype(ml_dtypes.bfloat16)
    return ropeT, masks, identB, identF, ptm


def run(T, x, W_qkv, b_qkv, w1_w, w1_s, w2_w, w2_s, h1_w, h1_s, h2_w, h2_s, w3_w, w3_s,
        dbg=False, want_raw=False):
    TOWN = T // 2
    nc = build_program(T, dbg=dbg)
    ropeT, masks, identB, identF, ptm = _host_consts(T)

    def colsh(v):
        return np.ascontiguousarray(np.asarray(v, dtype=np.float32).reshape(-1, 128).T)

    base = {
        "w1": np.ascontiguousarray(w1_w, dtype=np.float32),
        "w2": np.ascontiguousarray(w2_w, dtype=np.float32),
        "h1": np.ascontiguousarray(h1_w, dtype=np.float32),
        "h2": np.ascontiguousarray(h2_w, dtype=np.float32),
        "w3": np.ascontiguousarray(w3_w, dtype=np.float32),
        "s1c": colsh(w1_s), "s2c": colsh(w2_s),
        "sh1c": colsh(h1_s), "sh2c": colsh(h2_s),
        "s3b": np.ascontiguousarray(
            np.broadcast_to(np.asarray(w3_s, dtype=np.float32), (128, D))),
        "ropeT": ropeT, "masksI": masks,
        "identB": identB, "identF": identF, "ptmI": ptm,
    }
    in_maps = []
    for c in range(8):
        b, half = c // 2, c % 2
        r0 = half * 512
        m = dict(base)
        m["x_nat"] = np.ascontiguousarray(x[b], dtype=np.float32)
        m["x_own"] = np.ascontiguousarray(x[b, half * TOWN:(half + 1) * TOWN], dtype=np.float32)
        m["wq"] = np.ascontiguousarray(W_qkv[r0:r0 + 512], dtype=np.float32)
        m["wk"] = np.ascontiguousarray(W_qkv[1024 + r0:1024 + r0 + 512], dtype=np.float32)
        m["wv"] = np.ascontiguousarray(W_qkv[2048 + r0:2048 + r0 + 512], dtype=np.float32)
        m["bq"] = np.ascontiguousarray(
            np.asarray(b_qkv[r0:r0 + 512], dtype=np.float32).reshape(NPAIR, 128).T)
        m["bk"] = np.ascontiguousarray(
            np.asarray(b_qkv[1024 + r0:1024 + r0 + 512], dtype=np.float32).reshape(NPAIR, 128).T)
        m["bv"] = np.ascontiguousarray(
            np.asarray(b_qkv[2048 + r0:2048 + r0 + 512], dtype=np.float32).reshape(NPAIR, 128).T)
        m["mblI"] = np.ascontiguousarray(
            np.broadcast_to(np.array([1.0 - half, float(half)], dtype=np.float32), (128, 2)))
        in_maps.append(m)

    res = run_bass_kernel_spmd(nc, in_maps, core_ids=list(range(8)))
    out = np.empty((4, T, D), dtype=np.float32)
    for c in range(8):
        b, half = c // 2, c % 2
        out[b, half * TOWN:(half + 1) * TOWN] = res.results[c]["out_shard"]
    if want_raw:
        return out, res
    return out


def kernel(L0_attn_x, L0_attn_W_qkv, L0_attn_b_qkv, L0_ffn_w1_w, L0_ffn_w1_s,
           L0_ffn_w2_w, L0_ffn_w2_s, L0_ffn_h1_w, L0_ffn_h1_s, L0_ffn_h2_w,
           L0_ffn_h2_s, L0_ffn_w3_w, L0_ffn_w3_s):
    x = np.asarray(L0_attn_x, dtype=np.float32)
    return run(x.shape[1], x, np.asarray(L0_attn_W_qkv), np.asarray(L0_attn_b_qkv),
               np.asarray(L0_ffn_w1_w), np.asarray(L0_ffn_w1_s),
               np.asarray(L0_ffn_w2_w), np.asarray(L0_ffn_w2_s),
               np.asarray(L0_ffn_h1_w), np.asarray(L0_ffn_h1_s),
               np.asarray(L0_ffn_h2_w), np.asarray(L0_ffn_h2_s),
               np.asarray(L0_ffn_w3_w), np.asarray(L0_ffn_w3_s))


# version 07 (final)
# speedup vs baseline: 1.0820x; 1.0820x over previous
import numpy as np
import ml_dtypes
import concourse.bass as bass
import concourse.mybir as mybir
from concourse.bass_utils import run_bass_kernel_spmd

F32 = mybir.dt.float32
BF16 = mybir.dt.bfloat16
AF = mybir.ActivationFunctionType
OP = mybir.AluOpType

D = 1024
NKC = D // 128
HID = 4096
NHC = HID // 128
TQ = 512
NPAIR = 4
GT = 512
NM2 = NHC // 2
EPS = 1e-5

SEMS = ["SY", "LDX0", "LDX1", "XST0", "XST1", "RTB0", "RTB1", "WQL", "XNG",
        "W1S0", "W1S1", "W2S0", "W2S1", "WB0", "WB1", "CAND", "XOW",
        "SYT0", "SYT1", "GPS0", "GPS1", "GPP0", "GPP1", "LNS0", "LNS1",
        "GAT0", "GAT1", "DOUT", "VE", "TE", "SC", "GPC", "CCS", "PHZ"]


class Ctx:
    def __init__(self, eng=None, builder=None, plan=None, semh=None):
        self.eng = eng
        self.b = builder
        self.plan = plan
        self.semh = semh
        self.sem = {s: 0 for s in SEMS}
        self.marks = {} if plan is None else plan.marks

    SELF_SEM = {"vector": "VE", "scalar": "SC", "gpsimd": "GPC"}

    def op(self, eng, fn, inc=None, mark=None, waits=()):
        if self.eng == eng:
            for w in waits:
                self._wait(w)
            ss = self.SELF_SEM.get(eng)
            if ss is not None and self.sem[ss] > 0 and inc is not None and inc[0] == ss:
                self.b.wait_ge(self.semh[ss], self.sem[ss])
        ins = fn(self.b) if self.eng == eng else None
        if inc is not None:
            s, k = inc
            self.sem[s] += k
            if ins is not None:
                ins.then_inc(self.semh[s], k)
            if mark is not None and self.plan is None:
                assert mark not in self.marks, f"dup mark {mark}"
                self.marks[mark] = (s, self.sem[s])
        return ins

    def wait(self, eng, *ws):
        if self.eng == eng:
            for w in ws:
                self._wait(w)

    def _wait(self, w):
        if isinstance(w, tuple) and len(w) == 3 and w[0] == "raw":
            self.b.wait_ge(self.semh[w[1]], w[2])
        else:
            s, v = self.marks[w]
            self.b.wait_ge(self.semh[s], v)


def build_program(T, dbg=False):
    TOWN = T // 2
    NT = T // TQ
    NLN_A = T // 128
    NLN_D = TOWN // 128
    NG = TOWN // GT
    NT128 = GT // 128
    NWBU = 2 * NM2 + 4

    nc = bass.Bass("TRN2", target_bir_lowering=False, detect_race_conditions=False)

    def din(name, shape, dt=F32):
        return nc.dram_tensor(name, shape, dt, kind="ExternalInput")

    WSHAPE = {"wq": (512, D), "wk": (512, D), "wv": (512, D),
              "w1": (HID, D), "w2": (HID, D), "w3": (D, HID),
              "h1": (HID, HID), "h2": (HID, HID)}
    x_nat = din("x_nat", [T, D])
    x_own = din("x_own", [TOWN, D])
    wsrc = {k: din(k, list(v)) for k, v in WSHAPE.items()}
    bqI = din("bq", [128, NPAIR]); bkI = din("bk", [128, NPAIR]); bvI = din("bv", [128, NPAIR])
    s1I = din("s1c", [128, NHC]); s2I = din("s2c", [128, NHC])
    sh1I = din("sh1c", [128, NHC]); sh2I = din("sh2c", [128, NHC])
    s3bI = din("s3b", [128, D])
    ropeI = din("ropeT", [2, 128, T])
    masksI = din("masksI", [4, 128, TQ])
    identBI = din("identB", [128, 128], BF16)
    identFI = din("identF", [128, 128], F32)
    ptmI = din("ptmI", [128, 128], BF16)
    mblI = din("mblI", [128, 2])
    out_shard = nc.dram_tensor("out_shard", [TOWN, D], F32, kind="ExternalOutput")
    if dbg:
        dbg_attn = nc.dram_tensor("dbg_attn", [512, T], BF16, kind="ExternalOutput")
        dbg_gth = nc.dram_tensor("dbg_gth", [1024, T], BF16, kind="ExternalOutput")
        dbg_xnT = nc.dram_tensor("dbg_xnT", [D, T], BF16, kind="ExternalOutput")
        dbg_xnOT = nc.dram_tensor("dbg_xnOT", [D, TOWN], BF16, kind="ExternalOutput")

    xnT_scr = nc.dram_tensor("xnT_scr", [D, T], BF16)
    xnOT_scr = nc.dram_tensor("xnOT_scr", [D, TOWN], BF16)
    attn_scr = nc.dram_tensor("attn_scr", [512, T], BF16)
    gth = nc.dram_tensor("gth", [1024, T], BF16)
    bf_scr = {k: nc.dram_tensor(k + "_bf", list(v), BF16) for k, v in WSHAPE.items()}
    tr_scr = {}
    for k, v in WSHAPE.items():
        if k in ("wq", "wk", "wv"):
            tr_scr[k] = nc.dram_tensor(k + "T_s", [v[1], v[0]], BF16)
        else:
            tr_scr[k] = nc.dram_tensor(k + "T_s", [v[0] // 256, 128, v[1] // 128, 256], BF16)

    def sb(name, shape, dt, **kw):
        return nc.alloc_sbuf_tensor(name, shape, dt, **kw)

    ident = sb("identS", [128, 128], BF16)
    identF = sb("identFS", [128, 128], F32)
    ptm = sb("ptmS", [128, 128], BF16)
    bq_sb = sb("bq_sb", [128, NPAIR], F32)
    bk_sb = sb("bk_sb", [128, NPAIR], F32)
    bv_sb = sb("bv_sb", [128, NPAIR], F32)
    s1_sb = sb("s1_sb", [128, NHC], F32)
    s2_sb = sb("s2_sb", [128, NHC], F32)
    s12_sb = sb("s12_sb", [128, NHC], F32)
    sh1_sb = sb("sh1_sb", [128, NHC], F32)
    sh2_sb = sb("sh2_sb", [128, NHC], F32)
    s3b_sb = sb("s3b_sb", [128, D], F32)
    mbl_sb = sb("mbl_sb", [128, 2], F32)
    xln = [sb(f"xln{j}", [128, D], F32) for j in range(2)]
    sqd = sb("sqd", [128, D], F32)
    xnb = [sb(f"xnb{j}", [128, D], BF16) for j in range(2)]
    cols = sb("cols", [128, 16], F32)
    trstage = [sb(f"trstage{j}", [128, D], BF16) for j in range(2)]

    l0_top = nc.sbuf_base

    pt_b = [sb(f"pt_b{j}", [128, HID], BF16, is_transpose_dest=True) for j in range(2)]
    masks_sb = sb("masks_sb", [128, 4 * TQ], F32)
    xst = [sb(f"xst{j}", [128, NKC * TQ], BF16) for j in range(2)]
    rtb = [sb(f"rtb{j}", [128, 2 * TQ], F32) for j in range(2)]
    wqT_sb = sb("wqT_sb", [128, NKC * 128], BF16)
    wkT_sb = sb("wkT_sb", [128, NKC * 128], BF16)
    wvT_sb = sb("wvT_sb", [128, NKC * 128], BF16)
    qT = sb("qT", [128, T], BF16)
    kT = sb("kT", [128, T], BF16)
    v_sb = sb("v_sb", [128, T], BF16)
    q_sbuf = [sb(f"q_sbuf{j}", [128, TQ], BF16) for j in range(2)]
    k_sbuf = [sb(f"k_sbuf{j}", [128, TQ], BF16) for j in range(2)]
    vT_sbuf = [sb(f"vT_sbuf{j}", [128, TQ], F32) for j in range(2)]
    t1q = [sb(f"t1q{j}", [128, TQ], F32) for j in range(2)]
    t2q = [sb(f"t2q{j}", [128, TQ], F32) for j in range(2)]
    t1k = [sb(f"t1k{j}", [128, TQ], F32) for j in range(2)]
    t2k = [sb(f"t2k{j}", [128, TQ], F32) for j in range(2)]
    sA_sb = [sb(f"sA_sb{j}", [128, TQ], BF16) for j in range(2)]
    sB_sb = [sb(f"sB_sb{j}", [128, TQ], BF16) for j in range(2)]
    avst = [sb(f"avst{j}", [128, TQ], BF16) for j in range(2)]

    l2ptr = [l0_top - 16 * 1024]

    def sb2(name, shape, dt):
        nbytes = int(np.prod(shape[1:])) * (2 if dt == BF16 else 4)
        off = (l2ptr[0] + 31) // 32 * 32
        t = nc.alloc_sbuf_tensor_at(name, shape, dt, offset=off)
        l2ptr[0] = off + nbytes
        return t

    xng = [sb2("xng0", [128, NKC * GT], BF16)]
    hA = sb2("hA", [128, NHC * GT], BF16)
    hB = sb2("hB", [128, NHC * GT], BF16)
    w1s = [sb2(f"w1s{j}", [128, NKC * 256], BF16) for j in range(2)]
    w2s = [sb2(f"w2s{j}", [128, NKC * 256], BF16) for j in range(2)]
    wbig = [sb2(f"wbig{j}", [128, NHC * 256], BF16) for j in range(2)]
    xown_sb = [sb2(f"xown_sb{j}", [128, D], F32) for j in range(NT128)]
    candA = [sb2("candA0", [128, D], BF16)]
    candB = [sb2("candB0", [128, D], BF16)]
    bln = [sb2("bln0", [128, D], BF16)]
    atok = [sb2(f"atok{j}", [128, D], BF16) for j in range(NT128)]
    outst = [sb2(f"outst{j}", [128, D], F32) for j in range(NT128)]
    sg_sb = [sb2(f"sg_sb{j}", [128, GT], F32) for j in range(2)]
    tm1 = [sb2(f"tm1{j}", [128, GT], F32) for j in range(2)]
    o1b = [sb2(f"o1b{j}", [128, 256], F32) for j in range(2)]
    prsum = [sb2(f"prsum{j}", [128, D], BF16) for j in range(NT128)]
    assert l2ptr[0] <= 229376, f"L2 overflow {l2ptr[0]}"

    pb = [nc.alloc_psum_tensor(f"pb{i}", [128, 512], F32) for i in range(6)]
    ptr_ = [nc.alloc_psum_tensor("ptA", [128, 1024], BF16),
            nc.alloc_psum_tensor("ptB", [128, 1024], BF16)]

    semh = {s: nc.alloc_semaphore(s) for s in SEMS}

    def walk(c: Ctx):
        cl = [(ident, identBI), (identF, identFI), (ptm, ptmI), (bq_sb, bqI),
              (bk_sb, bkI), (bv_sb, bvI), (s1_sb, s1I), (s2_sb, s2I),
              (sh1_sb, sh1I), (sh2_sb, sh2I), (s3b_sb, s3bI), (mbl_sb, mblI)]
        for dst_, src_ in cl:
            c.op("sync", lambda sy, dst_=dst_, src_=src_: sy.dma_start(out=dst_[:], in_=src_[:]),
                 inc=("SY", 16))
        c.op("sync", lambda sy: sy.dma_start(
                out=masks_sb.rearrange("p (j t) -> p j t", j=4),
                in_=masksI.rearrange("j p t -> p j t")),
             inc=("SY", 16), mark="consts")
        c.op("vector", lambda v: v.tensor_mul(s12_sb[:], s1_sb[:], s2_sb[:]),
             inc=("VE", 1), mark="s12", waits=["consts"])

        prep_gp = []
        p1_done = {}
        p1_idx = 0
        for wn in ["wq", "wk", "wv", "w1", "w2", "w3", "h1", "h2"]:
            src, dst = wsrc[wn], bf_scr[wn]
            R, C = WSHAPE[wn]
            nrc = R // 128
            for rc in range(nrc):
                def f(g, src=src, dst=dst, rc=rc):
                    return g.dma_start(out=dst[rc * 128:(rc + 1) * 128, :],
                                       in_=src[rc * 128:(rc + 1) * 128, :])
                mk = f"p1_{wn}_{rc}" if rc >= nrc - 2 else None
                prep_gp.append((f, mk, p1_idx % 2))
                p1_idx += 1
            p1_done[wn] = [f"p1_{wn}_{nrc-2}", f"p1_{wn}_{nrc-1}"] if nrc >= 2 \
                else [f"p1_{wn}_{nrc-1}"]
        prep_sy = []
        p2_done = {}
        p2_end = {}
        p1_cum = 0
        p1_cum_map = {}
        for wn in ["wq", "wk", "wv", "w1", "w2", "w3", "h1", "h2"]:
            p1_cum += WSHAPE[wn][0] // 128
            p1_cum_map[wn] = p1_cum
        for wn in ["wq", "wk", "wv", "w1", "w2", "w3", "h1", "h2"]:
            dst_bf = bf_scr[wn]
            dstT = tr_scr[wn]
            R, C = WSHAPE[wn]
            for ic in range(C // 128):
                idx = len(prep_sy)
                j = idx % 2
                def fl(sy, dst_bf=dst_bf, ic=ic, j=j, R=R):
                    return sy.dma_start(out=pt_b[j][:, 0:R],
                                        in_=dst_bf[:, ic * 128:(ic + 1) * 128],
                                        transpose=True)
                if wn in ("wq", "wk", "wv"):
                    def fs(g, dstT=dstT, ic=ic, j=j, R=R):
                        return g.dma_start(out=dstT[ic * 128:(ic + 1) * 128, :],
                                           in_=pt_b[j][:, 0:R])
                else:
                    def fs(g, dstT=dstT, ic=ic, j=j, R=R):
                        return g.dma_start(
                            out=dstT[:, :, ic, :].rearrange("m p oc -> p m oc"),
                            in_=pt_b[j][:, 0:R].rearrange("p (m oc) -> p m oc", oc=256))
                lw = list(p1_done[wn])
                if idx >= 2:
                    lw.append(f"p2s_{idx-2}")
                prep_sy.append((fl, f"p2l_{idx}", lw, fs, f"p2s_{idx}", p1_cum_map[wn]))
                if wn not in p2_done:
                    p2_done[wn] = []
                p2_done[wn] = (p2_done[wn] + [f"p2s_{idx}"])[-2:]
                p2_end[wn] = idx + 1
        pst = {"gp": 0, "sy": 0}

        def drain_until(wn):
            while pst["sy"] < p2_end[wn]:
                before = (pst["gp"], pst["sy"])
                drain_prep(4, 4)
                assert (pst["gp"], pst["sy"]) != before, "prep drain stuck"

        def drain_prep(n_gp, n_sy):
            for _ in range(n_gp):
                i = pst["gp"]
                if i >= len(prep_gp):
                    break
                f, mk, par = prep_gp[i]
                sname = f"GPP{par}"
                thr = [("raw", sname, c.sem[sname])] if c.sem[sname] > 0 else []
                c.op("gpsimd", f, inc=(sname, 16), mark=mk, waits=thr)
                pst["gp"] += 1
            for _ in range(n_sy):
                i = pst["sy"]
                if i >= len(prep_sy):
                    break
                fl, mkl, lw, fs, mks, min_gp = prep_sy[i]
                if pst["gp"] < min_gp:
                    break
                c.op("sync", fl, inc=(f"SYT{i % 2}", 16), mark=mkl, waits=lw)
                c.op("gpsimd", fs, inc=(f"GPS{i % 2}", 16), mark=mks, waits=[mkl])
                pst["sy"] += 1

        def ln_tile(l, src_dram, row0, dst_scr, col0, sl, p2l_):
            j = sl % 2
            cS = cols[:, j * 8 + 0:j * 8 + 1]
            cQ = cols[:, j * 8 + 1:j * 8 + 2]
            cM = cols[:, j * 8 + 2:j * 8 + 3]
            cX = cols[:, j * 8 + 3:j * 8 + 4]
            cN = cols[:, j * 8 + 4:j * 8 + 5]
            cV = cols[:, j * 8 + 5:j * 8 + 6]
            cU = cols[:, j * 8 + 6:j * 8 + 7]
            cR = cols[:, j * 8 + 7:j * 8 + 8]
            lw = [f"ln_ap_{p2l_}"] if p2l_ is not None else ["consts"]
            c.op("sync", lambda sy: sy.dma_start(out=xln[j][:], in_=src_dram[row0:row0 + 128, :]),
                 inc=(f"LDX{j}", 16), mark=f"ln_ld_{l}", waits=lw)
            c.op("vector", lambda v: v.reduce_sum(cS, xln[j][:], mybir.AxisListType.X),
                 inc=("VE", 1), mark=f"ln_sum_{l}", waits=[f"ln_ld_{l}"])
            scw = [f"ln_ld_{l}"] + ([f"ln_cm_{p2l_}"] if p2l_ is not None else [])
            c.op("scalar", lambda s: s.activation(sqd[:], xln[j][:], AF.Square, accum_out=cQ),
                 inc=("SC", 1), mark=f"ln_sq_{l}", waits=scw)
            vcw = [f"ln_sq_{l}"] + ([f"ln_rs_{p2l_}", f"ln_ap_{p2l_}"] if p2l_ is not None else [])
            c.op("vector", lambda v: v.tensor_scalar_mul(cM, cS, 1.0 / D),
                 inc=("VE", 1), waits=vcw)
            c.op("vector", lambda v: v.tensor_scalar_mul(cN, cQ, 1.0 / D), inc=("VE", 1))
            c.op("vector", lambda v: v.tensor_mul(cX, cM, cM), inc=("VE", 1))
            c.op("vector", lambda v: v.tensor_sub(cV, cN, cX),
                 inc=("VE", 1), mark=f"ln_cm_{l}")
            rsw = [f"ln_cm_{l}"] + ([f"ln_ap_{p2l_}"] if p2l_ is not None else [])
            c.op("scalar", lambda s: s.activation(cU, cV, AF.Ln, bias=np.float32(EPS)),
                 inc=("SC", 1), waits=rsw)
            c.op("scalar", lambda s: s.activation(cR, cU, AF.Exp, scale=np.float32(-0.5)),
                 inc=("SC", 1), mark=f"ln_rs_{l}")
            apw = [f"ln_rs_{l}"] + ([f"ln_tr_{p2l_}"] if p2l_ is not None else [])
            c.op("vector",
                 lambda v: v.tensor_scalar(xnb[j][:], xln[j][:], cM, cR, OP.subtract, OP.mult),
                 inc=("VE", 1), mark=f"ln_ap_{l}", waits=apw)
            ptile = ptr_[j]
            trw = [f"ln_ap_{l}"] + ([f"ln_ev_{p2l_}"] if p2l_ is not None else [])

            def trans(t):
                ins = None
                for kc2 in range(NKC):
                    ins = t.transpose(ptile[:, kc2 * 128:(kc2 + 1) * 128],
                                      xnb[j][:, kc2 * 128:(kc2 + 1) * 128], ident[:])
                return ins
            c.op("tensor", trans, inc=("TE", 1), mark=f"ln_tr_{l}", waits=trw)
            evw = [f"ln_tr_{l}"] + ([f"ln_sc_{p2l_}"] if p2l_ is not None else [])
            c.op("vector", lambda v: v.tensor_copy(trstage[j][:], ptile[:]),
                 inc=("VE", 1), mark=f"ln_ev_{l}", waits=evw)
            c.op("gpsimd",
                 lambda g: g.dma_start(
                     out=dst_scr.rearrange("(kc p) t -> p kc t", p=128)[:, :, col0:col0 + 128],
                     in_=trstage[j].rearrange("p (kc t) -> p kc t", kc=NKC)),
                 inc=(f"LNS{j}", 16), mark=f"ln_sc_{l}", waits=[f"ln_ev_{l}"])

        ln_order = []
        di = 0
        for a in range(NLN_A):
            ln_order.append(a)
            if a % 2 == 1 and di < NLN_D:
                ln_order.append(NLN_A + di)
                di += 1
        while di < NLN_D:
            ln_order.append(NLN_A + di)
            di += 1
        for sl, l in enumerate(ln_order):
            p2l_ = ln_order[sl - 2] if sl >= 2 else None
            if l < NLN_A:
                ln_tile(l, x_nat, l * 128, xnT_scr, l * 128, sl, p2l_)
            else:
                dl = l - NLN_A
                ln_tile(l, x_own, dl * 128, xnOT_scr, dl * 128, sl, p2l_)
            drain_prep(1, 1)

        NSIG = [4 * tau + 4 for tau in range(NT)]

        def stage1_main(p, i):
            lt = p * NT + i
            jj = i % 2
            t0 = i * TQ
            xw = []
            if p == 0:
                xw.append(f"ln_sc_{4*i+3}")
            if lt >= 2:
                xw.append(f"b_vmm_{lt-2}")
            c.op("sync", lambda sy, jj=jj, t0=t0: sy.dma_start(
                    out=xst[jj].rearrange("p (kc t) -> p kc t", kc=NKC),
                    in_=xnT_scr.rearrange("(kc p) t -> p kc t", p=128)[:, :, t0:t0 + TQ]),
                 inc=(f"XST{jj}", 16), mark=f"b_xl_{lt}", waits=xw)
            rw = []
            if lt >= 2:
                rw += [f"b_t1k_{lt-2}", f"b_t2k_{lt-2}"]
            c.op("sync", lambda sy, jj=jj, t0=t0: sy.dma_start(
                    out=rtb[jj].rearrange("p (c t) -> p c t", c=2),
                    in_=ropeI[:, :, t0:t0 + TQ].rearrange("c p t -> p c t")),
                 inc=(f"RTB{jj}", 16), mark=f"b_rl_{lt}", waits=rw)
            qw = [f"b_xl_{lt}"]
            if i == 0:
                qw.append(f"b_wl_{p}")
            if lt >= 2:
                qw.append(f"b_t2q_{lt-2}")
            if i == 0 and p > 0:
                qw += [f"b_avevA_{p-1}_{NT-1}", f"b_avevB_{p-1}_{NT-1}",
                       f"b_seA_{p-1}_{NT-1}_{NSIG[NT-1]-5}"]

            def qmm(t, jj=jj):
                ins = None
                for kc in range(NKC):
                    ins = t.matmul(pb[jj][:], wqT_sb[:, kc * 128:(kc + 1) * 128],
                                   xst[jj][:, kc * TQ:(kc + 1) * TQ],
                                   start=(kc == 0), stop=(kc == NKC - 1))
                return ins
            c.op("tensor", qmm, inc=("TE", 1), mark=f"b_qmm_{lt}", waits=qw)
            kw_ = [f"b_t2k_{lt-2}"] if lt >= 2 else []

            def kmm(t, jj=jj):
                ins = None
                for kc in range(NKC):
                    ins = t.matmul(pb[2 + jj][:], wkT_sb[:, kc * 128:(kc + 1) * 128],
                                   xst[jj][:, kc * TQ:(kc + 1) * TQ],
                                   start=(kc == 0), stop=(kc == NKC - 1))
                return ins
            c.op("tensor", kmm, inc=("TE", 1), mark=f"b_kmm_{lt}", waits=kw_)
            vw = [f"b_vtrev_{lt-2}"] if lt >= 2 else []

            def vmm(t, jj=jj):
                ins = None
                for kc in range(NKC):
                    ins = t.matmul(pb[4 + jj][:], wvT_sb[:, kc * 128:(kc + 1) * 128],
                                   xst[jj][:, kc * TQ:(kc + 1) * TQ],
                                   start=(kc == 0), stop=(kc == NKC - 1))
                return ins
            c.op("tensor", vmm, inc=("TE", 1), mark=f"b_vmm_{lt}", waits=vw)
            ew1 = [f"b_qmm_{lt}"] + ([f"b_t1q_{lt-2}"] if lt >= 2 else [])
            c.op("scalar", lambda s, jj=jj, p=p: s.activation(
                    q_sbuf[jj][:], pb[jj][:], AF.Identity, bias=bq_sb[:, p:p + 1]),
                 inc=("SC", 1), mark=f"b_qe_{lt}", waits=ew1)
            ew2 = [f"b_kmm_{lt}"] + ([f"b_t1k_{lt-2}"] if lt >= 2 else [])
            c.op("scalar", lambda s, jj=jj, p=p: s.activation(
                    k_sbuf[jj][:], pb[2 + jj][:], AF.Identity, bias=bk_sb[:, p:p + 1]),
                 inc=("SC", 1), mark=f"b_ke_{lt}", waits=ew2)
            ew3 = [f"b_vmm_{lt}"] + ([f"b_vtr_{lt-2}"] if lt >= 2 else [])
            c.op("scalar", lambda s, jj=jj, p=p: s.activation(
                    vT_sbuf[jj][:], pb[4 + jj][:], AF.Identity, bias=bv_sb[:, p:p + 1]),
                 inc=("SC", 1), mark=f"b_ve_{lt}", waits=ew3)

        def stage1_lag(p, i):
            lt = p * NT + i
            jj = i % 2
            t0 = i * TQ
            c.op("tensor", lambda t, jj=jj: t.matmul(
                    pb[jj][:], ptm[:], q_sbuf[jj][:], start=True, stop=True),
                 inc=("TE", 1), mark=f"b_pq_{lt}", waits=[f"b_qe_{lt}"])
            c.op("tensor", lambda t, jj=jj: t.matmul(
                    pb[2 + jj][:], ptm[:], k_sbuf[jj][:], start=True, stop=True),
                 inc=("TE", 1), mark=f"b_pk_{lt}", waits=[f"b_ke_{lt}"])

            def vtr(t, jj=jj):
                ins = None
                for cb in range(TQ // 128):
                    ins = t.transpose(pb[4 + jj][:, cb * 128:(cb + 1) * 128],
                                      vT_sbuf[jj][:, cb * 128:(cb + 1) * 128], identF[:])
                return ins
            c.op("tensor", vtr, inc=("TE", 1), mark=f"b_vtr_{lt}", waits=[f"b_ve_{lt}"])
            gw = [f"b_qe_{lt}", f"b_rl_{lt}"] + ([f"b_qT_{lt-2}"] if lt >= 2 else [])
            c.op("gpsimd", lambda g, jj=jj: g.tensor_mul(
                    t1q[jj][:], q_sbuf[jj][:], rtb[jj][:, 0:TQ]),
                 inc=("GPC", 1), mark=f"b_t1q_{lt}", waits=gw)
            vw = [f"b_pq_{lt}", f"b_rl_{lt}"] + ([f"b_qT_{lt-2}"] if lt >= 2 else [])
            c.op("vector", lambda v, jj=jj: v.tensor_mul(
                    t2q[jj][:], pb[jj][:], rtb[jj][:, TQ:2 * TQ]),
                 inc=("VE", 1), mark=f"b_t2q_{lt}", waits=vw)
            c.op("gpsimd", lambda g, jj=jj, t0=t0: g.tensor_add(
                    qT[:, t0:t0 + TQ], t1q[jj][:], t2q[jj][:]),
                 inc=("GPC", 1), mark=f"b_qT_{lt}", waits=[f"b_t2q_{lt}"])
            gw = [f"b_ke_{lt}", f"b_rl_{lt}"] + ([f"b_kT_{lt-2}"] if lt >= 2 else [])
            c.op("gpsimd", lambda g, jj=jj: g.tensor_mul(
                    t1k[jj][:], k_sbuf[jj][:], rtb[jj][:, 0:TQ]),
                 inc=("GPC", 1), mark=f"b_t1k_{lt}", waits=gw)
            vw = [f"b_pk_{lt}"] + ([f"b_kT_{lt-2}"] if lt >= 2 else [])
            c.op("vector", lambda v, jj=jj: v.tensor_mul(
                    t2k[jj][:], pb[2 + jj][:], rtb[jj][:, TQ:2 * TQ]),
                 inc=("VE", 1), mark=f"b_t2k_{lt}", waits=vw)
            c.op("gpsimd", lambda g, jj=jj, t0=t0: g.tensor_add(
                    kT[:, t0:t0 + TQ], t1k[jj][:], t2k[jj][:]),
                 inc=("GPC", 1), mark=f"b_kT_{lt}", waits=[f"b_t2k_{lt}"])
            c.op("vector", lambda v, jj=jj, t0=t0: v.tensor_copy(
                    v_sb[:, t0:t0 + TQ], pb[4 + jj][:]),
                 inc=("VE", 1), mark=f"b_vtrev_{lt}", waits=[f"b_vtr_{lt}"])

        drain_until("wv")
        for p in range(NPAIR):
            wlw = p2_done["wq"] + p2_done["wk"] + p2_done["wv"]
            if p > 0:
                wlw.append(f"b_vmm_{p*NT-1}")
            c.op("sync", lambda sy, p=p: sy.dma_start(
                    out=wqT_sb.rearrange("p (kc o) -> p kc o", kc=NKC),
                    in_=tr_scr["wq"].rearrange("(kc p) o -> p kc o", p=128)[:, :, p * 128:(p + 1) * 128]),
                 inc=("WQL", 16), waits=wlw)
            c.op("sync", lambda sy, p=p: sy.dma_start(
                    out=wkT_sb.rearrange("p (kc o) -> p kc o", kc=NKC),
                    in_=tr_scr["wk"].rearrange("(kc p) o -> p kc o", p=128)[:, :, p * 128:(p + 1) * 128]),
                 inc=("WQL", 16))
            c.op("sync", lambda sy, p=p: sy.dma_start(
                    out=wvT_sb.rearrange("p (kc o) -> p kc o", kc=NKC),
                    in_=tr_scr["wv"].rearrange("(kc p) o -> p kc o", p=128)[:, :, p * 128:(p + 1) * 128]),
                 inc=("WQL", 16), mark=f"b_wl_{p}")

            for i in range(NT + 1):
                if i < NT:
                    stage1_main(p, i)
                if i > 0:
                    stage1_lag(p, i - 1)
                drain_prep(1, 1)

            lt_last = p * NT + NT - 1
            st_entry = [f"b_vtrev_{lt_last}", f"b_t2q_{lt_last}", f"b_t2k_{lt_last}"]

            for tau in range(NT):
                nsig = NSIG[tau]
                tq0 = tau * TQ
                tw = [f"b_qT_{p*NT+tau}", f"b_kT_{p*NT+tau}", f"b_vtrev_{p*NT+tau}"]
                if tau == 0:
                    tw += st_entry
                if tau >= 1:
                    tw += [f"b_avevA_{p}_{tau-1}", f"b_avevB_{p}_{tau-1}"]
                elif p > 0:
                    tw += [f"b_avevA_{p-1}_{NT-1}", f"b_avevB_{p-1}_{NT-1}"]

                def av_pair(t, sig, tau=tau):
                    sj = sig % 2
                    t.matmul(pb[4][0:64, :], v_sb[:, sig * 128:sig * 128 + 64],
                             sA_sb[sj][:], start=(sig == 0), stop=(sig == NSIG[tau] - 1),
                             tile_position=(0, 0))
                    return t.matmul(pb[5][64:128, :],
                                    v_sb[:, sig * 128 + 64:sig * 128 + 128],
                                    sB_sb[sj][:], start=(sig == 0), stop=(sig == NSIG[tau] - 1),
                                    tile_position=(0, 64))

                for sig in range(nsig):
                    sj = sig % 2
                    stw = list(tw) if sig == 0 else []
                    if sig == 1 and tau == 0:
                        stw = list(st_entry)
                    if sig >= 2:
                        stw += [f"b_seA_{p}_{tau}_{sig-2}", f"b_seB_{p}_{tau}_{sig-2}"]
                    elif tau > 0:
                        ps_ = NSIG[tau - 1]
                        stw += [f"b_seA_{p}_{tau-1}_{ps_-2+sig}", f"b_seB_{p}_{tau-1}_{ps_-2+sig}"]

                    def st_pair(t, sig=sig, tq0=tq0):
                        sj2 = sig % 2
                        t.matmul(pb[sj2][:], kT[0:64, sig * 128:(sig + 1) * 128],
                                 qT[0:64, tq0:tq0 + TQ], start=True, stop=True,
                                 tile_position=(0, 0))
                        return t.matmul(pb[2 + sj2][:], kT[64:128, sig * 128:(sig + 1) * 128],
                                        qT[64:128, tq0:tq0 + TQ], start=True, stop=True,
                                        tile_position=(64, 0))
                    c.op("tensor", st_pair, inc=("TE", 1), mark=f"b_st_{p}_{tau}_{sig}",
                         waits=stw)
                    is_diag = sig >= nsig - 4
                    jm = sig - (nsig - 4)
                    ew = [f"b_st_{p}_{tau}_{sig}"]
                    if sig >= 2:
                        ew.append(f"b_av_{p}_{tau}_{sig-2}")
                    elif tau > 0:
                        ew.append(f"b_av_{p}_{tau-1}_{NSIG[tau-1]-2+sig}")
                    elif p > 0:
                        ew.append(f"b_av_{p-1}_{NT-1}_{NSIG[NT-1]-2+sig}")
                    if is_diag:
                        c.op("vector", lambda v, sj=sj, jm=jm: v.tensor_mul(
                                sA_sb[sj][:], pb[sj][:],
                                masks_sb[:, jm * TQ:(jm + 1) * TQ]),
                             inc=("VE", 1), mark=f"b_seA_{p}_{tau}_{sig}", waits=ew)
                        c.op("vector", lambda v, sj=sj, jm=jm: v.tensor_mul(
                                sB_sb[sj][:], pb[2 + sj][:],
                                masks_sb[:, jm * TQ:(jm + 1) * TQ]),
                             inc=("VE", 1), mark=f"b_seB_{p}_{tau}_{sig}", waits=[])
                    else:
                        c.op("scalar", lambda s, sj=sj: s.copy(sA_sb[sj][:], pb[sj][:]),
                             inc=("SC", 1), mark=f"b_seA_{p}_{tau}_{sig}", waits=ew)
                        c.op("vector", lambda v, sj=sj: v.tensor_copy(sB_sb[sj][:], pb[2 + sj][:]),
                             inc=("VE", 1), mark=f"b_seB_{p}_{tau}_{sig}", waits=ew)
                    if sig > 0:
                        c.op("tensor", lambda t, sig=sig: av_pair(t, sig - 1),
                             inc=("TE", 1), mark=f"b_av_{p}_{tau}_{sig-1}",
                             waits=[f"b_seA_{p}_{tau}_{sig-1}", f"b_seB_{p}_{tau}_{sig-1}"])
                c.op("tensor", lambda t, nsig=nsig: av_pair(t, nsig - 1),
                     inc=("TE", 1), mark=f"b_av_{p}_{tau}_{nsig-1}",
                     waits=[f"b_seA_{p}_{tau}_{nsig-1}", f"b_seB_{p}_{tau}_{nsig-1}"])
                evw = [f"b_av_{p}_{tau}_{nsig-1}"]
                if tau >= 2:
                    evw.append(f"b_gat_{p}_{tau-2}")
                elif p > 0:
                    evw.append(f"b_gat_{p-1}_{NT-2+tau}")
                c.op("vector", lambda v, tau=tau: v.tensor_copy(
                        avst[tau % 2][0:64, :], pb[4][0:64, :]),
                     inc=("VE", 1), mark=f"b_avevA_{p}_{tau}", waits=evw)
                c.op("scalar", lambda sc_, tau=tau: sc_.copy(
                        avst[tau % 2][64:128, :], pb[5][64:128, :]),
                     inc=("SC", 1), mark=f"b_avevB_{p}_{tau}", waits=evw)
                c.op("gpsimd", lambda g, tau=tau, p=p, tq0=tq0: g.dma_start(
                        out=attn_scr[p * 128:(p + 1) * 128, tq0:tq0 + TQ],
                        in_=avst[tau % 2][:]),
                     inc=(f"GAT{tau % 2}", 16), mark=f"b_gat_{p}_{tau}",
                     waits=[f"b_avevA_{p}_{tau}", f"b_avevB_{p}_{tau}"])
                drain_prep(2, 2)

        c.op("gpsimd", lambda g: g.collective_compute(
                "AllGather", OP.bypass,
                replica_groups=[[0, 1], [2, 3], [4, 5], [6, 7]],
                ins=[attn_scr[:]], outs=[gth[:]]),
             inc=("CCS", 1), mark="cc_done",
             waits=[("raw", "GAT0", c.sem["GAT0"]), ("raw", "GAT1", c.sem["GAT1"])])

        drain_prep(10**9, 10**9)

        for eng in ["sync", "vector", "tensor", "scalar", "gpsimd"]:
            c.op(eng, lambda e: e.nop(), inc=("PHZ", 1))
        bar = ("raw", "PHZ", 5)
        first_e = {e: True for e in ["sync", "vector", "tensor", "scalar", "gpsimd"]}

        def ew_(eng, waits):
            w = list(waits)
            if first_e[eng]:
                w.insert(0, bar)
                first_e[eng] = False
            return w

        def wbu_idx(g, kind, i):
            return g * NWBU + {"h1": 0, "h2": NM2, "w3": 2 * NM2}[kind] + i

        for g in range(NG):
            gj = 0
            xw = [f"ln_sc_{NLN_A + g*4 + 3}"]
            if g >= 1:
                xw.append(f"e_w2mm_{g-1}_{NM2-1}")
            c.op("sync", lambda sy, gj=gj, g=g: sy.dma_start(
                    out=xng[gj].rearrange("p (kc t) -> p kc t", kc=NKC),
                    in_=xnOT_scr.rearrange("(kc p) t -> p kc t", p=128)[:, :, g * GT:(g + 1) * GT]),
                 inc=("XNG", 16), mark=f"e_xn_{g}", waits=ew_("sync", xw))

            for m2 in range(NM2):
                mj = m2 % 2
                widx = g * NM2 + m2
                wl = list(p2_done["w1"]) if widx == 0 else []
                if widx >= 2:
                    wl.append(f"e_w1mm_{(widx-2)//NM2}_{(widx-2)%NM2}")
                c.op("sync", lambda sy, mj=mj, m2=m2: sy.dma_start(
                        out=w1s[mj].rearrange("p (kc o) -> p kc o", kc=NKC),
                        in_=tr_scr["w1"][m2]),
                     inc=(f"W1S{mj}", 16), mark=f"e_w1l_{g}_{m2}", waits=ew_("sync", wl))
                wl = list(p2_done["w2"]) if widx == 0 else []
                if widx >= 2:
                    wl.append(f"e_w2mm_{(widx-2)//NM2}_{(widx-2)%NM2}")
                c.op("sync", lambda sy, mj=mj, m2=m2: sy.dma_start(
                        out=w2s[mj].rearrange("p (kc o) -> p kc o", kc=NKC),
                        in_=tr_scr["w2"][m2]),
                     inc=(f"W2S{mj}", 16), mark=f"e_w2l_{g}_{m2}", waits=wl)

                mw = [f"e_w1l_{g}_{m2}", f"e_xn_{g}"]
                if widx >= 1:
                    pg, pm = divmod(widx - 1, NM2)
                    mw.append(f"e_ve_t_{pg}_{pm}_1")
                if m2 < 2 and g > 0:
                    mw.append(f"e_hC_{g-1}_{NHC-1}")

                def w1mm(t, gj=gj, mj=mj):
                    ins = None
                    for mm in range(2):
                        for kc in range(NKC):
                            ins = t.matmul(pb[mm][:],
                                           w1s[mj][:, kc * 256 + mm * 128:kc * 256 + mm * 128 + 128],
                                           xng[gj][:, kc * GT:(kc + 1) * GT],
                                           start=(kc == 0), stop=(kc == NKC - 1))
                    return ins
                c.op("tensor", w1mm, inc=("TE", 1), mark=f"e_w1mm_{g}_{m2}",
                     waits=ew_("tensor", mw))

                mw = [f"e_w2l_{g}_{m2}"]
                if widx >= 1:
                    pg, pm = divmod(widx - 1, NM2)
                    mw.append(f"e_hA_{pg}_{pm*2+1}")
                if m2 < 2 and g > 0:
                    mw.append(f"e_hC_{g-1}_{NHC-1}")

                def w2mm(t, gj=gj, mj=mj):
                    ins = None
                    for mm in range(2):
                        for kc in range(NKC):
                            ins = t.matmul(pb[2 + mm][:],
                                           w2s[mj][:, kc * 256 + mm * 128:kc * 256 + mm * 128 + 128],
                                           xng[gj][:, kc * GT:(kc + 1) * GT],
                                           start=(kc == 0), stop=(kc == NKC - 1))
                    return ins
                c.op("tensor", w2mm, inc=("TE", 1), mark=f"e_w2mm_{g}_{m2}", waits=mw)
                for mm in range(2):
                    m = m2 * 2 + mm
                    sw = [f"e_w1mm_{g}_{m2}"]
                    if g > 0 and m == 0:
                        sw.append(f"e_wbu_mm_{wbu_idx(g-1, 'w3', 3)}")
                    c.op("scalar", lambda s, mm=mm, m=m: s.activation(
                            sg_sb[mm][:], pb[mm][:], AF.Sigmoid, scale=s1_sb[:, m:m + 1]),
                         inc=("SC", 1), mark=f"e_sg_{g}_{m2}_{mm}", waits=ew_("scalar", sw))
                    c.op("vector", lambda v, mm=mm, m=m: v.scalar_tensor_tensor(
                            tm1[mm][:], pb[mm][:], s12_sb[:, m:m + 1], sg_sb[mm][:],
                            OP.mult, OP.mult),
                         inc=("VE", 1), mark=f"e_ve_t_{g}_{m2}_{mm}",
                         waits=ew_("vector", [f"e_sg_{g}_{m2}_{mm}"]))
                    c.op("vector", lambda v, mm=mm, m=m: v.tensor_mul(
                            hA[:, m * GT:(m + 1) * GT], tm1[mm][:], pb[2 + mm][:]),
                         inc=("VE", 1), mark=f"e_hA_{g}_{m}",
                         waits=[f"e_w2mm_{g}_{m2}"])

            for m2 in range(NM2):
                u = wbu_idx(g, "h1", m2)
                mj = u % 2
                wl = p1_done["h1"] + p2_done["h1"] if u == 0 else []
                if u >= 2:
                    wl.append(f"e_wbu_mm_{u-2}")
                c.op("sync", lambda sy, mj=mj, m2=m2: sy.dma_start(
                        out=wbig[mj].rearrange("p (kc o) -> p kc o", kc=NHC),
                        in_=tr_scr["h1"][m2]),
                     inc=(f"WB{u % 2}", 16), mark=f"e_wbl_{u}", waits=wl)
                mw = [f"e_wbl_{u}", f"e_hA_{g}_{NHC-1}"]
                if m2 >= 2:
                    mw += [f"e_hB_{g}_{(m2-2)*2}", f"e_hB_{g}_{(m2-2)*2+1}"]

                def h1mm(t, mj=mj, m2=m2):
                    ins = None
                    for mm in range(2):
                        bk_ = (m2 % 2) * 2 + mm
                        for kc in range(NHC):
                            ins = t.matmul(pb[bk_][:],
                                           wbig[mj][:, kc * 256 + mm * 128:kc * 256 + mm * 128 + 128],
                                           hA[:, kc * GT:(kc + 1) * GT],
                                           start=(kc == 0), stop=(kc == NHC - 1))
                    return ins
                c.op("tensor", h1mm, inc=("TE", 1), mark=f"e_wbu_mm_{u}", waits=mw)
                for mm in range(2):
                    m = m2 * 2 + mm
                    bk_ = (m2 % 2) * 2 + mm
                    sw = [f"e_wbu_mm_{u}"]
                    if g > 0 and m == 0:
                        sw.append(f"e_wbu_mm_{wbu_idx(g-1, 'h2', NM2-1)}")
                    c.op("scalar", lambda s, bk_=bk_, m=m: s.activation(
                            hB[:, m * GT:(m + 1) * GT], pb[bk_][:], AF.Gelu,
                            scale=sh1_sb[:, m:m + 1]),
                         inc=("SC", 1), mark=f"e_hB_{g}_{m}", waits=sw)

            for m2 in range(NM2):
                u = wbu_idx(g, "h2", m2)
                mj = u % 2
                wl = p1_done["h2"] + p2_done["h2"] if g == 0 and m2 == 0 else []
                if u >= 2:
                    wl.append(f"e_wbu_mm_{u-2}")
                c.op("sync", lambda sy, mj=mj, m2=m2: sy.dma_start(
                        out=wbig[mj].rearrange("p (kc o) -> p kc o", kc=NHC),
                        in_=tr_scr["h2"][m2]),
                     inc=(f"WB{u % 2}", 16), mark=f"e_wbl_{u}", waits=wl)
                mw = [f"e_wbl_{u}", f"e_hB_{g}_{NHC-1}"]
                if m2 >= 2:
                    mw += [f"e_hC_{g}_{(m2-2)*2}", f"e_hC_{g}_{(m2-2)*2+1}"]

                def h2mm(t, mj=mj, m2=m2):
                    ins = None
                    for mm in range(2):
                        bk_ = (m2 % 2) * 2 + mm
                        for kc in range(NHC):
                            ins = t.matmul(pb[bk_][:],
                                           wbig[mj][:, kc * 256 + mm * 128:kc * 256 + mm * 128 + 128],
                                           hB[:, kc * GT:(kc + 1) * GT],
                                           start=(kc == 0), stop=(kc == NHC - 1))
                    return ins
                c.op("tensor", h2mm, inc=("TE", 1), mark=f"e_wbu_mm_{u}", waits=mw)
                for mm in range(2):
                    m = m2 * 2 + mm
                    bk_ = (m2 % 2) * 2 + mm
                    sw = [f"e_wbu_mm_{u}"]
                    if m == 0:
                        sw.append(f"e_wbu_mm_{wbu_idx(g, 'h1', NM2-1)}")
                    c.op("scalar", lambda s, bk_=bk_, m=m: s.activation(
                            hA[:, m * GT:(m + 1) * GT], pb[bk_][:], AF.Gelu,
                            scale=sh2_sb[:, m:m + 1]),
                         inc=("SC", 1), mark=f"e_hC_{g}_{m}", waits=sw)

            for t_ in range(NT128):
                lidx = g * NT128 + t_
                tj = 0
                tcol = g * GT + t_ * 128
                lw = []
                if lidx == 0:
                    lw.append("cc_done")
                if lidx >= 1:
                    lw.append(f"e_bln_{lidx-1}")
                c.op("sync", lambda sy, tj=tj, tcol=tcol: sy.dma_start(
                        out=candA[tj].rearrange("p (kc t) -> p kc t", kc=NKC),
                        in_=gth.rearrange("(kc p) t -> p kc t", p=128)[:, :, tcol:tcol + 128]),
                     inc=("CAND", 16), mark=f"e_cA_{lidx}", waits=lw)
                c.op("sync", lambda sy, tj=tj, tcol=tcol: sy.dma_start(
                        out=candB[tj].rearrange("p (kc t) -> p kc t", kc=NKC),
                        in_=gth.rearrange("(kc p) t -> p kc t", p=128)[:, :, TOWN + tcol:TOWN + tcol + 128]),
                     inc=("CAND", 16), mark=f"e_cB_{lidx}", waits=[])
                gw = [f"e_cA_{lidx}", f"e_cB_{lidx}"]
                if lidx >= 1:
                    gw.append(f"e_ptr_{lidx-1}")

                c.op("vector", lambda v, tj=tj: v.tensor_scalar_mul(
                        bln[tj][:], candA[tj][:], mbl_sb[:, 0:1]),
                     inc=("VE", 1), waits=ew_("vector", gw))
                c.op("vector", lambda v, tj=tj: v.scalar_tensor_tensor(
                        bln[tj][:], candB[tj][:], mbl_sb[:, 1:2],
                        bln[tj][:], OP.mult, OP.add),
                     inc=("VE", 1), mark=f"e_bln_{lidx}")
                ptile = ptr_[lidx % 2]
                tw2 = [f"e_bln_{lidx}"]
                if lidx >= 2:
                    tw2.append(f"e_atok_{lidx-2}")

                def atr(t, ptile=ptile, tj=tj):
                    ins = None
                    for kc in range(NKC):
                        ins = t.transpose(ptile[:, kc * 128:(kc + 1) * 128],
                                          bln[tj][:, kc * 128:(kc + 1) * 128], ident[:])
                    return ins
                c.op("tensor", atr, inc=("TE", 1), mark=f"e_ptr_{lidx}", waits=tw2)
                aw = [f"e_ptr_{lidx}"]
                if g > 0:
                    aw.append(f"e_o3_{g-1}_{t_}_3")
                c.op("vector", lambda v, ptile=ptile, t_=t_: v.tensor_copy(atok[t_][:], ptile[:]),
                     inc=("VE", 1), mark=f"e_atok_{lidx}", waits=aw)
                xw2 = []
                if g > 0:
                    xw2.append(f"e_pre_{g-1}_{t_}")
                c.op("sync", lambda sy, t_=t_, tcol=tcol: sy.dma_start(
                        out=xown_sb[t_][:], in_=x_own[tcol:tcol + 128, :]),
                     inc=("XOW", 16), mark=f"e_xo_{g}_{t_}", waits=xw2)
                pw = [("raw", "XOW", 16 * (4 * g + t_ + 1)), f"e_atok_{lidx}"]
                if g > 0:
                    pw.append(f"e_o3_{g-1}_{t_}_3")
                c.op("gpsimd", lambda gp, t_=t_: gp.tensor_add(
                        prsum[t_][:], xown_sb[t_][:], atok[t_][:]),
                     inc=("GPC", 1), mark=f"e_pre_{g}_{t_}", waits=pw)

            for qtr in range(4):
                u = wbu_idx(g, "w3", qtr)
                qj = u % 2
                wl = p1_done["w3"] + p2_done["w3"] if g == 0 and qtr == 0 else []
                if u >= 2:
                    wl.append(f"e_wbu_mm_{u-2}")
                c.op("sync", lambda sy, qj=qj, qtr=qtr: sy.dma_start(
                        out=wbig[qj].rearrange("p (kc o) -> p kc o", kc=NHC),
                        in_=tr_scr["w3"][qtr]),
                     inc=(f"WB{u % 2}", 16), mark=f"e_wbl_{u}", waits=wl)
                for t_ in range(NT128):
                    li2 = g * 16 + qtr * 4 + t_
                    bk_ = 4 + li2 % 2

                    def w3mm(t, qj=qj, t_=t_, bk_=bk_):
                        ins = None
                        for kc in range(NHC):
                            ins = t.matmul(pb[bk_][:, 0:256],
                                           hA[:, kc * GT + t_ * 128:kc * GT + t_ * 128 + 128],
                                           wbig[qj][:, kc * 256:(kc + 1) * 256],
                                           start=(kc == 0), stop=(kc == NHC - 1))
                        return ins
                    mw = [f"e_wbl_{u}", f"e_hC_{g}_{NHC-1}"]
                    if li2 >= 2:
                        mw.append(f"e_o1i_{li2-2}")
                    c.op("tensor", w3mm, inc=("TE", 1), mark=f"e_w3m_{li2}", waits=mw)
                    if t_ == NT128 - 1 and c.plan is None:
                        c.marks[f"e_wbu_mm_{u}"] = c.marks[f"e_w3m_{li2}"]
                    oj = li2 % 2
                    dcol = qtr * 256
                    c.op("vector", lambda v, oj=oj, bk_=bk_, dcol=dcol: v.tensor_mul(
                            o1b[oj][:], pb[bk_][:, 0:256], s3b_sb[:, dcol:dcol + 256]),
                         inc=("VE", 1), mark=f"e_o1i_{li2}", waits=[f"e_w3m_{li2}"])
                    o2w = [f"e_pre_{g}_{t_}"]
                    if li2 >= 2:
                        pg2, r2 = divmod(li2 - 2, 16)
                        o2w.append(f"e_o3_{pg2}_{r2%4}_{r2//4}")
                    if g > 0 and qtr == 0:
                        o2w.append(("raw", "DOUT", 16 * 4 * g))
                    c.op("vector", lambda v, oj=oj, t_=t_, dcol=dcol: v.tensor_add(
                            outst[t_][:, dcol:dcol + 256], o1b[oj][:],
                            prsum[t_][:, dcol:dcol + 256]),
                         inc=("VE", 1), mark=f"e_o3_{g}_{t_}_{qtr}", waits=o2w)
            for t_ in range(NT128):
                trow = g * GT + t_ * 128
                c.op("sync", lambda sy, t_=t_, trow=trow: sy.dma_start(
                        out=out_shard[trow:trow + 128, :], in_=outst[t_][:]),
                     inc=("DOUT", 16), mark=f"e_out_{g}_{t_}",
                     waits=[f"e_o3_{g}_{t_}_{q}" for q in range(4)])

        if dbg:
            c.wait("sync", ("raw", "DOUT", 16 * NG * NT128))
            c.op("sync", lambda sy: sy.dma_start(out=dbg_attn[:], in_=attn_scr[:]),
                 inc=("DOUT", 16))
            c.op("sync", lambda sy: sy.dma_start(out=dbg_gth[:], in_=gth[:]),
                 inc=("DOUT", 16))
            c.op("sync", lambda sy: sy.dma_start(out=dbg_xnT[:], in_=xnT_scr[:]),
                 inc=("DOUT", 16))
            c.op("sync", lambda sy: sy.dma_start(out=dbg_xnOT[:], in_=xnOT_scr[:]),
                 inc=("DOUT", 16))
            c.wait("sync", ("raw", "DOUT", 16 * (NG * NT128 + 4)))
        else:
            c.wait("sync", ("raw", "DOUT", 16 * NG * NT128))

    plan = Ctx(semh=semh)
    walk(plan)

    with nc.Block() as block:
        block.sync(lambda e: walk(Ctx("sync", e, plan, semh)))
        block.vector(lambda e: walk(Ctx("vector", e, plan, semh)))
        block.tensor(lambda e: walk(Ctx("tensor", e, plan, semh)))
        block.scalar(lambda e: walk(Ctx("scalar", e, plan, semh)))
        block.gpsimd(lambda e: walk(Ctx("gpsimd", e, plan, semh)))

    return nc



def _host_consts(T):
    DH = 64
    inv = 1.0 / (10000.0 ** (np.arange(0, DH, 2, dtype=np.float64) / DH))
    ang = np.arange(T, dtype=np.float64)[:, None] * inv[None, :]
    cos = np.cos(ang); sin = np.sin(ang)
    cos64 = np.concatenate([cos, cos], axis=1).T
    sin64 = np.concatenate([sin, sin], axis=1).T
    ropeT = np.stack([np.tile(cos64, (2, 1)), np.tile(sin64, (2, 1))], axis=0).astype(np.float32)
    masks = np.zeros((4, 128, TQ), dtype=np.float32)
    for j in range(4):
        s = np.arange(128)[:, None] + 128 * j
        t = np.arange(TQ)[None, :]
        masks[j] = (s <= t).astype(np.float32)
    identB = np.eye(128, dtype=np.float32).astype(ml_dtypes.bfloat16)
    identF = np.eye(128, dtype=np.float32)
    P64 = np.zeros((64, 64), dtype=np.float32)
    for d in range(32):
        P64[d, d + 32] = -1.0
        P64[d + 32, d] = 1.0
    P = np.zeros((128, 128), dtype=np.float32)
    P[0:64, 0:64] = P64
    P[64:128, 64:128] = P64
    ptm = P.T.astype(ml_dtypes.bfloat16)
    return ropeT, masks, identB, identF, ptm


def run(T, x, W_qkv, b_qkv, w1_w, w1_s, w2_w, w2_s, h1_w, h1_s, h2_w, h2_s, w3_w, w3_s,
        dbg=False, want_raw=False):
    TOWN = T // 2
    nc = build_program(T, dbg=dbg)
    ropeT, masks, identB, identF, ptm = _host_consts(T)

    def colsh(v):
        return np.ascontiguousarray(np.asarray(v, dtype=np.float32).reshape(-1, 128).T)

    base = {
        "w1": np.ascontiguousarray(w1_w, dtype=np.float32),
        "w2": np.ascontiguousarray(w2_w, dtype=np.float32),
        "h1": np.ascontiguousarray(h1_w, dtype=np.float32),
        "h2": np.ascontiguousarray(h2_w, dtype=np.float32),
        "w3": np.ascontiguousarray(w3_w, dtype=np.float32),
        "s1c": colsh(w1_s), "s2c": colsh(w2_s),
        "sh1c": colsh(h1_s), "sh2c": colsh(h2_s),
        "s3b": np.ascontiguousarray(
            np.broadcast_to(np.asarray(w3_s, dtype=np.float32), (128, D))),
        "ropeT": ropeT, "masksI": masks,
        "identB": identB, "identF": identF, "ptmI": ptm,
    }
    in_maps = []
    for c in range(8):
        b, half = c // 2, c % 2
        r0 = half * 512
        m = dict(base)
        m["x_nat"] = np.ascontiguousarray(x[b], dtype=np.float32)
        m["x_own"] = np.ascontiguousarray(x[b, half * TOWN:(half + 1) * TOWN], dtype=np.float32)
        m["wq"] = np.ascontiguousarray(W_qkv[r0:r0 + 512], dtype=np.float32)
        m["wk"] = np.ascontiguousarray(W_qkv[1024 + r0:1024 + r0 + 512], dtype=np.float32)
        m["wv"] = np.ascontiguousarray(W_qkv[2048 + r0:2048 + r0 + 512], dtype=np.float32)
        m["bq"] = np.ascontiguousarray(
            np.asarray(b_qkv[r0:r0 + 512], dtype=np.float32).reshape(NPAIR, 128).T)
        m["bk"] = np.ascontiguousarray(
            np.asarray(b_qkv[1024 + r0:1024 + r0 + 512], dtype=np.float32).reshape(NPAIR, 128).T)
        m["bv"] = np.ascontiguousarray(
            np.asarray(b_qkv[2048 + r0:2048 + r0 + 512], dtype=np.float32).reshape(NPAIR, 128).T)
        m["mblI"] = np.ascontiguousarray(
            np.broadcast_to(np.array([1.0 - half, float(half)], dtype=np.float32), (128, 2)))
        in_maps.append(m)

    res = run_bass_kernel_spmd(nc, in_maps, core_ids=list(range(8)))
    out = np.empty((4, T, D), dtype=np.float32)
    for c in range(8):
        b, half = c // 2, c % 2
        out[b, half * TOWN:(half + 1) * TOWN] = res.results[c]["out_shard"]
    if want_raw:
        return out, res
    return out


def kernel(L0_attn_x, L0_attn_W_qkv, L0_attn_b_qkv, L0_ffn_w1_w, L0_ffn_w1_s,
           L0_ffn_w2_w, L0_ffn_w2_s, L0_ffn_h1_w, L0_ffn_h1_s, L0_ffn_h2_w,
           L0_ffn_h2_s, L0_ffn_w3_w, L0_ffn_w3_s):
    x = np.asarray(L0_attn_x, dtype=np.float32)
    return run(x.shape[1], x, np.asarray(L0_attn_W_qkv), np.asarray(L0_attn_b_qkv),
               np.asarray(L0_ffn_w1_w), np.asarray(L0_ffn_w1_s),
               np.asarray(L0_ffn_w2_w), np.asarray(L0_ffn_w2_s),
               np.asarray(L0_ffn_h1_w), np.asarray(L0_ffn_h1_s),
               np.asarray(L0_ffn_h2_w), np.asarray(L0_ffn_h2_s),
               np.asarray(L0_ffn_w3_w), np.asarray(L0_ffn_w3_s))
